# version 22
# baseline (speedup 1.0000x reference)
import numpy as np
from contextlib import ExitStack

import concourse.bass as bass
import concourse.mybir as mybir
from concourse.bass_utils import run_bass_kernel_spmd

F32 = mybir.dt.float32
BF16 = mybir.dt.bfloat16
F16 = mybir.dt.float16
U8 = mybir.dt.uint8
AF = mybir.ActivationFunctionType
ALU = mybir.AluOpType

W = 8
D = 2048
NCH = 16
HEADS = 16
I_FF = 5632
KI = 44
OP = 44
GU_PER_CORE = 12
DN_PER_CORE = 2
GUB = 1024
DNB = 2816
GU_BLK = D * 128
DN_BLK = I_FF * 128
SLICE = GU_PER_CORE * GU_BLK + DN_PER_CORE * DN_BLK
LN_EPS = 1e-5
RMS_EPS = 1e-6
NLN2 = -float(np.log(2.0))


class Sem:
    def __init__(self, h):
        self.h = h
        self.v = 0


class B:

    def __init__(self, nc, ec):
        self.nc = nc
        self.ops = {e: [] for e in ("sync", "vector", "scalar", "tensor", "gpsimd")}
        self.sem = {}
        self.ec = ec

    def S(self, name):
        if name not in self.sem:
            self.sem[name] = Sem(self.ec(self.nc.semaphore(name)))
        return self.sem[name]

    def emit(self, eng, fn, waits=(), incs=(), auto=True):
        own = {"vector": "sv", "scalar": "ss", "gpsimd": "sgc"}.get(eng)
        w2 = [(s, v) for (s, v) in waits if v > 0]
        i2 = []
        if auto and own:
            so = self.S(own)
            w2.append((so, so.v))
            so.v += 1
            i2.append((so, 1))
        for (s, k) in incs:
            s.v += k
            i2.append((s, k))

        def run(e, w2=w2, i2=i2, fn=fn):
            for s, v in w2:
                e.wait_ge(s.h, v)
            ins = fn(e)
            for s, k in i2:
                ins.then_inc(s.h, k)

        self.ops[eng].append(run)

    def wait(self, eng, waits):
        w2 = [(s, v) for (s, v) in waits if v > 0]

        def run(e, w2=w2):
            for s, v in w2:
                e.wait_ge(s.h, v)

        self.ops[eng].append(run)


def build_program(T, dbg=False):
    TH = T // 2
    nc = bass.Bass("TRN2", target_bir_lowering=False)

    x_in = nc.dram_tensor("x_in", [T, D], F32, kind="ExternalInput").ap()
    gub_in = nc.dram_tensor("gub_in", [GU_PER_CORE * 128, GUB], U8, kind="ExternalInput").ap()
    gus_in = nc.dram_tensor("gus_in", [GU_PER_CORE * 128, 16], F16, kind="ExternalInput").ap()
    dnb_in = nc.dram_tensor("dnb_in", [DN_PER_CORE * 128, DNB], U8, kind="ExternalInput").ap()
    dns_in = nc.dram_tensor("dns_in", [DN_PER_CORE * 128, KI], F16, kind="ExternalInput").ap()
    gw_in = nc.dram_tensor("gw_in", [D, 128], F32, kind="ExternalInput").ap()
    rw_in = nc.dram_tensor("rw_in", [D, 128], F32, kind="ExternalInput").ap()
    gb_in = nc.dram_tensor("gb_in", [HEADS, 128], F32, kind="ExternalInput").ap()
    rb_in = nc.dram_tensor("rb_in", [HEADS, 128], F32, kind="ExternalInput").ap()
    rp_in = nc.dram_tensor("rp_in", [HEADS, 128], F32, kind="ExternalInput").ap()
    fix_in = nc.dram_tensor("fix_in", [128, 2], F32, kind="ExternalInput").ap()
    zm_in = nc.dram_tensor("zm_in", [128, W], F32, kind="ExternalInput").ap()
    y_out = nc.dram_tensor("y_out", [T, D], F32, kind="ExternalOutput").ap()

    if dbg:
        dbg_d = {k: nc.dram_tensor(f"dbg_{k}", [128, NCH * T], F32, kind="ExternalOutput").ap()
                 for k in ("xres0", "xn1", "h0", "x1", "xn2", "x2", "xn3")}
        dbgh_d = nc.dram_tensor("dbg_h", [128, 4 * T], F32, kind="ExternalOutput").ap()
        dbgw_d = nc.dram_tensor("dbg_w", [128, 4 * 2048 + 2 * 2816], BF16, kind="ExternalOutput").ap()
    GUPS = [2 * GU_BLK, 2 * GU_BLK, 2 * GU_BLK, 4 * GU_BLK, 2 * GU_BLK]
    DNSL = DN_PER_CORE * DN_BLK
    wslice_gu = [nc.dram_tensor(f"wslice_gu{i}", [GUPS[i]], BF16, kind="Internal").ap()
                 for i in range(5)]
    wslice_dn = nc.dram_tensor("wslice_dn", [DNSL], BF16, kind="Internal").ap()
    wfull_gu = [nc.dram_tensor(f"wfull_gu{i}", [W * GUPS[i]], BF16, kind="Internal").ap()
                for i in range(5)]
    wfull_dn = nc.dram_tensor("wfull_dn", [W * DNSL], BF16, kind="Internal").ap()
    xsp_d = nc.dram_tensor("xsp_d", [NCH, 128, T], F32, kind="Internal").ap()
    cg_in_d = [nc.dram_tensor(f"cg_in{g}", [128, 16], F32, kind="Internal").ap() for g in range(2)]
    cg_out_d = [nc.dram_tensor(f"cg_out{g}", [W * 128, 16], F32, kind="Internal").ap() for g in range(2)]

    def gu_pair_ap(pair, which):
        c, k = pair % W, pair // W
        if k < 3:
            piece, off = k, (c * 2 + which) * GU_BLK
        elif k < 5:
            piece, off = 3, (c * 4 + (k - 3) * 2 + which) * GU_BLK
        else:
            piece, off = 4, (c * 2 + which) * GU_BLK
        return wfull_gu[piece][off : off + GU_BLK].rearrange("(p f) -> p f", p=128)

    def dn_blk_ap(dc, kb):
        off = dc * DN_BLK
        blk = wfull_dn[off : off + DN_BLK].rearrange("(p f) -> p f", p=128)
        return blk[:, kb * (I_FF // 2) : (kb + 1) * (I_FF // 2)]

    stack = ExitStack()
    ec = stack.enter_context
    bb = B(nc, ec)
    E, EW, S = bb.emit, bb.wait, bb.S

    xres = ec(nc.sbuf_tensor("xres", [128, NCH * T], F32))
    U = ec(nc.sbuf_tensor("U", [128, 17 * T], F32))
    xnbf = ec(nc.sbuf_tensor("xnbf", [128, NCH * T], BF16))
    wblk = ec(nc.sbuf_tensor("wblk", [128, 8192], BF16))
    mf = ec(nc.sbuf_tensor("mf", [128, 2 * T], F32))
    mb = ec(nc.sbuf_tensor("mb", [128, 3 * T], BF16))
    sb = ec(nc.sbuf_tensor("sb", [128, T], BF16))
    aux = ec(nc.sbuf_tensor("aux", [128, T], F32))
    aux2 = ec(nc.sbuf_tensor("aux2", [128, T], F32))
    ident = ec(nc.sbuf_tensor("ident", [128, 128], BF16))
    identf = ec(nc.sbuf_tensor("identf", [128, 128], F32))
    ones_c = ec(nc.sbuf_tensor("ones_c", [128, 1], BF16))
    ones_r = ec(nc.sbuf_tensor("ones_r", [1, 128], BF16))
    consts = ec(nc.sbuf_tensor("consts", [128, 4], F32))
    gvec = ec(nc.sbuf_tensor("gvec", [128, 6 * HEADS], F32))
    zg = ec(nc.sbuf_tensor("zg", [128, W * 32], F32))
    zsc = ec(nc.sbuf_tensor("zsc", [128, 48], F32))
    zmt = ec(nc.sbuf_tensor("zmt", [128, W], F32))
    cgt = ec(nc.sbuf_tensor("cgt", [128, 32], F32))
    bias_s = ec(nc.sbuf_tensor("bias_s", [16, 128], F32))
    scl = ec(nc.sbuf_tensor("scl", [128, 280], F16))
    ps_all = ec(nc.psum_tensor("ps_all", [128, 4096], F32))
    ps = [ps_all[:, i * 512 : (i + 1) * 512] for i in range(8)]

    def psw(i):
        return ps_all[:, i * 512 : (i + 2) * 512]

    def ps2(i):
        return ps_all[:, i * 512 : (i + 2) * 512].rearrange(
            "p (b c) -> p b c", b=2)[:, :, 0:TH]

    def r2(ap):
        return ap.rearrange("p (b c) -> p b c", b=2)

    small = T < 1024
    if small:
        decR = ec(nc.sbuf_tensor("decR", [128, 71680], U8))
        xtmR = [ec(nc.sbuf_tensor(f"xtm{i}", [128, 2048], F32)) for i in range(2)]
        gatesR = ec(nc.sbuf_tensor("gatesR", [128, 4096], BF16))
        xoldR = [ec(nc.sbuf_tensor(f"xold{i}", [128, T], F32)) for i in range(2)]
        bscrR = [ec(nc.sbuf_tensor(f"bscr{i}", [128, T], F32)) for i in range(2)]
        ytmR = [ec(nc.sbuf_tensor(f"ytm{i}", [128, 2048], F32)) for i in range(2)]
    Ub = U.bitcast(U8)
    xnbfB = xnbf.bitcast(U8)
    xresB = xres.bitcast(U8)
    wblkB = wblk.bitcast(U8)

    def xres_c(j, half=None):
        if half is None:
            return xres[:, j * T : (j + 1) * T]
        return xres[:, j * T + half * TH : j * T + (half + 1) * TH]

    def uslot(i, half=None):
        if half is None:
            return U[:, i * T : (i + 1) * T]
        return U[:, i * T + half * TH : i * T + (half + 1) * TH]

    def xnbf_c(j, half=None):
        if half is None:
            return xnbf[:, j * T : (j + 1) * T]
        return xnbf[:, j * T + half * TH : j * T + (half + 1) * TH]

    def b1slot(h):
        return (h + 16) % 17

    def hview(ki, half=None):
        base = Ub if ki < 22 else xresB
        k2 = ki if ki < 22 else ki - 22
        v = base[:, k2 * 2 * T : (k2 + 1) * 2 * T].bitcast(BF16)
        if half is None:
            return v
        return v[:, half * TH : (half + 1) * TH]

    def yview(d, half=None):
        if d < 8:
            v = xnbfB[:, d * 4 * T : (d + 1) * 4 * T].bitcast(F32)
        elif d < 11:
            v = uslot(11 + (d - 8))
        else:
            v = xres_c(d)
        if half is None:
            return v
        return v[:, half * TH : (half + 1) * TH]

    def la_f16(h):
        return xnbfB[:, h * 2 * T : (h + 1) * 2 * T].bitcast(F16)

    def P_bf(h):
        return xnbfB[:, h * 2 * T : (h + 1) * 2 * T].bitcast(BF16)

    sv, ss, sgc = S("sv"), S("ss"), S("sgc")
    st_ = S("st_")
    sqx = [S("sqx0"), S("sqx1")]
    sqb = S("sqb")
    sqw = [S("sqw0"), S("sqw1")]
    sqo = [S("sqo0"), S("sqo1")]
    scq = S("scq")
    sqsc = S("sqsc")
    ssp = S("ssp")
    sgd = S("sgd")
    scc = S("scc")
    sout = [S("sout0"), S("sout1")]

    E("gpsimd", lambda e: e.memset(consts[:, 0:1], -2.0))
    E("gpsimd", lambda e: e.memset(consts[:, 1:2], -6.0))
    E("gpsimd", lambda e: e.memset(consts[:, 2:3], 4.0))
    E("gpsimd", lambda e: e.memset(consts[:, 3:4], 64.0))
    E("gpsimd", lambda e: e.memset(ident[:], 0.0))
    E("gpsimd", lambda e: e.affine_select(
        out=ident[:], in_=ident[:], compare_op=ALU.not_equal,
        fill=1.0, base=0, pattern=[[-1, 128]], channel_multiplier=1))
    E("gpsimd", lambda e: e.memset(identf[:], 0.0))
    E("gpsimd", lambda e: e.affine_select(
        out=identf[:], in_=identf[:], compare_op=ALU.not_equal,
        fill=1.0, base=0, pattern=[[-1, 128]], channel_multiplier=1))
    E("vector", lambda e: e.memset(ones_c[:], 1.0))
    E("vector", lambda e: e.memset(ones_r[:], 1.0))
    CONSTS = [(sgc, sgc.v), (sv, sv.v)]

    NTT = T // 128
    if small:
        xtm = xtmR
    else:
        xtm = [U[:, (12 + 2 * i) * T : (12 + 2 * i) * T + 2048] for i in range(2)]

    E("sync", lambda e: e.dma_start(
        out=scl[:, 0:192].rearrange("p (c g) -> p c g", c=12),
        in_=gus_in.rearrange("(c p) g -> p c g", p=128)),
        incs=[(sqsc, 16)], auto=False)
    E("sync", lambda e: e.dma_start(
        out=scl[:, 192:280].rearrange("p (c g) -> p c g", c=2),
        in_=dns_in.rearrange("(c p) g -> p c g", p=128)),
        incs=[(sqsc, 16)], auto=False)
    SCL_W = sqsc.v

    st0 = st_.v
    tile_done_inc = {}
    cnt = 0
    for r in range(max(NTT, 8)):
        if r < NTT:
            cnt += 4
        tile_done_inc[r] = cnt
        if 0 <= r - 2 < 6:
            cnt += 4
    xld_v = {}
    for r in range(NTT):
        buf = r % 2
        w = [(st_, st0 + tile_done_inc[r - 2])] if r >= 2 else []
        E("gpsimd" if r in (1, 3) else "sync",
          lambda e, r=r, buf=buf: e.dma_start(
            out=xtm[buf][:], in_=x_in[r * 128 : (r + 1) * 128, :]),
            waits=w, incs=[(sqx[buf], 16)], auto=False)
        xld_v[r] = sqx[buf].v
    xevac_v = [sv.v]

    def emit_xtile(r):
        buf = r % 2
        ld_v = xld_v[r]
        ev = xevac_v[0]
        for grp in range(4):
            for j in range(4):
                ch = grp * 4 + j
                wt = []
                if j == 0:
                    wt = [(sqx[buf], ld_v), (sv, ev)] + (CONSTS if (r == 0 and grp == 0) else [])
                E("tensor", lambda e, buf=buf, ch=ch, j=j: e.transpose(
                    ps[7][:, j * 128 : (j + 1) * 128],
                    xtm[buf][:, ch * 128 : (ch + 1) * 128], identf[:]),
                    waits=wt, incs=[(st_, 1)] if j == 3 else [], auto=False)
            for j in range(4):
                ch = grp * 4 + j
                E("vector", lambda e, r=r, ch=ch, j=j: e.tensor_copy(
                    xres[:, ch * T + r * 128 : ch * T + (r + 1) * 128],
                    ps[7][:, j * 128 : (j + 1) * 128]),
                    waits=[(st_, st_.v)] if j == 0 else [])
            ev = sv.v
        xevac_v[0] = ev

    dec_base = 0

    def dview(off, nbytes, dt):
        if small:
            v = decR[:, off : off + nbytes]
        else:
            v = Ub[:, off : off + nbytes]
        return v if dt == U8 else v.bitcast(dt)

    def decode_compute(nb, ngrp, bytes_src, scales_src, extra_wait, base, first=False,
                       ext_sct=None, bufv=None):
        if bufv is not None:
            def dv(off, nbytes, dt):
                v = bufv[:, off : off + nbytes]
                return v if dt == U8 else v.bitcast(dt)
        else:
            dv = dview
        ncdec = 2 * nb
        o = base
        bt = dv(o, nb, U8); o += nb
        k_u = dv(o, nb, U8); o += nb
        b3 = dv(o, nb, U8); o += nb
        o = (o + 3) & ~3
        r1 = dv(o, 2 * nb, BF16); o += 2 * nb
        r2 = dv(o, 2 * nb, BF16); o += 2 * nb
        t1 = dv(o, 2 * nb, BF16); o += 2 * nb
        sgn = dv(o, 2 * nb, BF16); o += 2 * nb
        L_ = dv(o, 2 * nb, BF16); o += 2 * nb
        v1 = dv(o, 2 * nb, BF16); o += 2 * nb
        wdec = dv(o, 4 * nb, BF16); o += 4 * nb
        stg = dv(o, 4 * nb, BF16); o += 4 * nb
        sct = dv(o, 2 * ((ngrp + 1) & ~1), F16)

        E("scalar", lambda e: e.dma_start(out=bt[:], in_=bytes_src),
          waits=list(extra_wait), incs=[(scq, 16)], auto=False)
        if ext_sct is None:
            E("scalar", lambda e: e.dma_start(out=sct[:, 0:ngrp], in_=scales_src),
              incs=[(scq, 16)], auto=False)
        else:
            sct = ext_sct
        ld_v = scq.v

        for cls in range(2):
            kmask = 0x07 if cls == 0 else 0x70
            smask = 0x08 if cls == 0 else 0x80
            kscale = 0.5 if cls == 0 else (1.0 / 32.0)
            r1s = 0.5 if cls == 0 else (1.0 / 32.0)
            r2s = 1.0 if cls == 0 else (1.0 / 16.0)
            E("vector", lambda e, m=kmask: e.tensor_scalar(k_u[:], bt[:], m, None, ALU.bitwise_and),
              waits=[(scq, ld_v), (ss, ss.v)])
            E("vector", lambda e, m=smask: e.tensor_scalar(b3[:], bt[:], m, None, ALU.bitwise_and))
            gm = sv.v
            E("scalar", lambda e, cls=cls: e.activation(sgn[:], b3[:], AF.Sign, scale=-1.0,
                                                        bias=consts[:, 2 + cls : 3 + cls]),
              waits=[(sv, gm)] + (CONSTS if (first and cls == 0) else []))
            E("scalar", lambda e, s=r1s: e.activation(r1[:], k_u[:], AF.Relu, scale=s, bias=consts[:, 0:1]))
            E("scalar", lambda e, s=r2s: e.activation(r2[:], k_u[:], AF.Relu, scale=s, bias=consts[:, 1:2]))
            E("vector", lambda e: e.tensor_tensor(t1[:], r1[:], r2[:], ALU.add),
              waits=[(ss, ss.v)])
            E("vector", lambda e, s=kscale: e.scalar_tensor_tensor(L_[:], k_u[:], s, t1[:], ALU.mult, ALU.add))
            E("vector", lambda e: e.tensor_tensor(v1[:], L_[:], sgn[:], ALU.mult))
            rep = nb // ngrp
            E("vector", lambda e, cls=cls, rep=rep: e.tensor_tensor(
                wdec[:, cls : ncdec : 2].rearrange("p (g r) -> p g r", g=ngrp),
                v1[:].rearrange("p (g r) -> p g r", g=ngrp),
                sct[:, 0:ngrp].unsqueeze(2).broadcast_to([128, ngrp, rep]),
                ALU.mult),
              waits=[(scq, ld_v)] + ([(sqsc, SCL_W)] if (ext_sct is not None and cls == 0) else []))
        return {"wdec": wdec, "stg": stg, "nt": ncdec // 128, "wdec_v": sv.v}

    def decode_trans(hd, blk_elems, slice_off, store_d, store_ap=None):
        wdec, stg, nt = hd["wdec"], hd["stg"], hd["nt"]
        pbank = [ps[6].bitcast(BF16), ps[4].bitcast(BF16)]
        ev_after = [ss.v, ss.v]
        prev_store = sgd.v
        for grp in range((nt + 3) // 4):
            bk = grp % 2
            tn = min(4, nt - grp * 4)
            for j in range(tn):
                t = grp * 4 + j
                wt = []
                if j == 0:
                    wt = [(sv, hd["wdec_v"]), (ss, ev_after[bk])] + (CONSTS if grp == 0 else [])
                E("tensor", lambda e, t=t, j=j, bk=bk: e.transpose(
                    pbank[bk][:, j * 128 : (j + 1) * 128],
                    wdec[:, t * 128 : (t + 1) * 128], ident[:]),
                    waits=wt, incs=[(st_, 1)] if j == tn - 1 else [], auto=False)
            E("scalar", lambda e, grp=grp, tn=tn, bk=bk: e.copy(
                stg[:, grp * 512 : grp * 512 + tn * 128],
                pbank[bk][:, 0 : tn * 128]),
                waits=[(st_, st_.v)] + ([(sgd, prev_store)] if grp == 0 else []))
            ev_after[bk] = ss.v
        out_ap = store_ap if store_ap is not None else \
            store_d[slice_off : slice_off + blk_elems].rearrange("(p f) -> p f", p=128)
        E("gpsimd", lambda e: e.dma_start(out=out_ap, in_=stg[:, 0 : blk_elems // 128]),
            waits=[(ss, ss.v), (sgd, sgd.v)], incs=[(sgd, 16)], auto=False)

    dec_marks = {0: [], 1: [], 2: []}
    W_READY = {}
    W2_BOX = {}
    dec_buf3 = (decR[:, 47744:71368] if small else xnbfB[:, 0:24064])

    dec_hd = {}

    def emit_dec_compute(lc):
        bs = lc % 3
        dec_hd[lc] = decode_compute(
            GUB, 16, gub_in[lc * 128 : (lc + 1) * 128, :], None,
            dec_marks[bs], base=(0, 24064, 0)[bs], first=(lc == 0),
            ext_sct=scl[:, lc * 16 : (lc + 1) * 16],
            bufv=dec_buf3 if bs == 2 else None)

    def emit_dec_trans(lc):
        decode_trans(dec_hd.pop(lc), GU_BLK, (lc % 2) * GU_BLK, wslice_gu[lc // 2])
        dec_marks[lc % 3] = [(sv, sv.v), (ss, ss.v), (st_, st_.v), (sgd, sgd.v)]
        if lc % 2 == 1:
            rd = lc // 2
            E("gpsimd", lambda e, rd=rd: e.collective_compute(
                "AllGather", ALU.bypass, ins=[wslice_gu[rd]], outs=[wfull_gu[rd]],
                replica_groups=[list(range(W))]),
                waits=[(sgd, sgd.v)], incs=[(scc, 1)], auto=False)
            W_READY[rd] = scc.v

    for r in range(max(NTT, 8)):
        if r < NTT:
            emit_xtile(r)
        if 0 <= r - 2 < 6:
            emit_dec_trans(r - 2)
        if r < 6:
            emit_dec_compute(r)
    XRES_V = xevac_v[0]
    XTM_DONE_T = st_.v

    INJ_BASE = 45056
    INJ_B2 = INJ_BASE + 11840
    inj_hd = [None, None]

    def mk_inj_compute(nb, ngrp, bsrc, sct_ap, base=INJ_BASE, slot=0):
        def f():
            ew = [(sv, sv.v), (ss, ss.v), (st_, st_.v), (sgc, sgc.v), (sgd, sgd.v)]
            inj_hd[slot] = decode_compute(nb, ngrp, bsrc, None, ew, base=base,
                                          ext_sct=sct_ap)
        return f

    def mk_inj_trans(blk_elems, off, store_d, store_ap=None, slot=0):
        def f():
            decode_trans(inj_hd[slot], blk_elems, off, store_d, store_ap)
        return f

    def mk_inj_ag(src, dst, box, key):
        def f():
            E("gpsimd", lambda e: e.collective_compute(
                "AllGather", ALU.bypass, ins=[src], outs=[dst],
                replica_groups=[list(range(W))]),
                waits=[(sgd, sgd.v)], incs=[(scc, 1)], auto=False)
            box[key] = scc.v
        return f

    SPB = {"v": 0}

    def build_inject():
        inj = {}

        def add(p, fn):
            inj.setdefault(p, []).append(fn)

        def mk_spill(j):
            def f():
                E("scalar", lambda e, j=j: e.dma_start(out=xsp_d[j], in_=xres_c(j)),
                  waits=[(sv, XRES_V)], incs=[(ssp, 16)], auto=False)
                SPB["v"] = ssp.v
            return f

        for j in range(NCH):
            add(2 + j, mk_spill(j))

        for i2 in range(12):
            lc = 6 + i2 // 2
            hh = i2 % 2
            sl = i2 % 2
            base = INJ_BASE if sl == 0 else INJ_B2
            add(i2, mk_inj_compute(
                GUB // 2, 8,
                gub_in[lc * 128 : (lc + 1) * 128, hh * 512 : (hh + 1) * 512],
                scl[:, lc * 16 + hh * 8 : lc * 16 + hh * 8 + 8],
                base=base, slot=sl))
            pc, po = (3, lc - 6) if lc < 10 else (4, lc - 10)
            full2d = wslice_gu[pc][po * GU_BLK : (po + 1) * GU_BLK].rearrange(
                "(p f) -> p f", p=128)
            add(i2 + 2, mk_inj_trans(GU_BLK // 2, 0, None,
                                     full2d[:, hh * 1024 : (hh + 1) * 1024], slot=sl))
        add(9, mk_inj_ag(wslice_gu[3], wfull_gu[3], W_READY, 3))
        add(13, mk_inj_ag(wslice_gu[4], wfull_gu[4], W_READY, 4))
        PCS = [6, 6, 6, 6, 6, 6, 4, 4]
        for j in range(16):
            lc, q = j // 8, j % 8
            cum = sum(PCS[:q])
            g = PCS[q]
            sl = j % 2
            base = INJ_BASE if sl == 0 else INJ_B2
            add(14 + j, mk_inj_compute(
                64 * g, g,
                dnb_in[lc * 128 : (lc + 1) * 128, cum * 64 : (cum + g) * 64],
                scl[:, 192 + lc * 44 + cum : 192 + lc * 44 + cum + g],
                base=base, slot=sl))
            chunk2d = wslice_dn[lc * DN_BLK : (lc + 1) * DN_BLK].rearrange(
                "(p f) -> p f", p=128)
            add(16 + j, mk_inj_trans(g * 128 * 128, 0, wslice_dn,
                                     chunk2d[:, cum * 128 : (cum + g) * 128], slot=sl))
        add(31, mk_inj_ag(wslice_dn, wfull_dn, W2_BOX, 0))
        return inj

    E("sync", lambda e: e.dma_start(out=zmt[:], in_=zm_in), incs=[(sqb, 16)])
    E("sync", lambda e: e.dma_start(out=gvec[:, 4 * HEADS : 4 * HEADS + 2], in_=fix_in),
      incs=[(sqb, 16)])
    bsrc = [gb_in, rb_in, rp_in]
    for i in range(3):
        wt = [(st_, st_.v)] if i > 0 else []
        E("sync", lambda e, i=i: e.dma_start(out=bias_s[:], in_=bsrc[i]),
          waits=wt, incs=[(sqb, 16)])
        wt2 = [(sqb, sqb.v)] + (CONSTS + [(sv, sv.v)] if i == 0 else [])
        E("tensor", lambda e, i=i: e.matmul(
            ps[5][:, i * 16 : (i + 1) * 16], bias_s[:],
            identf[0:16, 0:16], start=True, stop=True, skip_group_check=True),
            waits=wt2, incs=[(st_, 1)], auto=False)
    BIAS_LD = sqb.v
    E("vector", lambda e: e.tensor_copy(gvec[:, 0 : 3 * HEADS], ps[5][:, 0 : 3 * HEADS]),
      waits=[(st_, st_.v)])
    GVEC_T = sv.v
    E("scalar", lambda e: e.activation(gvec[:, 3 * HEADS : 4 * HEADS],
                                       gvec[:, 2 * HEADS : 3 * HEADS], AF.Exp),
      waits=[(sv, GVEC_T)])
    E("scalar", lambda e: e.activation(gvec[:, 3 * HEADS : 4 * HEADS],
                                       gvec[:, 3 * HEADS : 4 * HEADS], AF.Ln, bias=1.0))
    E("vector", lambda e: e.tensor_scalar_mul(gvec[:, 3 * HEADS : 4 * HEADS],
                                              gvec[:, 3 * HEADS : 4 * HEADS], -8.0),
      waits=[(ss, ss.v)])
    MSP_V = sv.v

    mrow = mf[0:1, :]
    mbrow = mb[0:1, :]

    def ln_begin():
        return {"stat_done": {}, "last": 0}

    def ln_chunk(st, j, extra_wt):
        sd = st["stat_done"]
        bq = j % 2
        xb = mb[:, bq * T : (bq + 1) * T]
        E("vector", lambda e, j=j, xb=xb: e.tensor_copy(xb, xres_c(j)),
          waits=([(st_, sd[j - 2])] if j >= 2 else []) + list(extra_wt))
        cast_v = sv.v
        sqv = mb[:, 2 * T : 3 * T]
        E("scalar", lambda e, j=j, sqv=sqv: e.activation(sqv, xres_c(j), AF.Square),
          waits=([(st_, sd[j - 1])] if j >= 1 else []) + list(extra_wt))
        sq_v = ss.v
        for half in range(2):
            wt = [(sv, cast_v), (ss, sq_v)] if half == 0 else []
            E("tensor", lambda e, j=j, half=half, xb=xb: e.matmul(
                ps[half][0:1, 0:TH], ones_c[:],
                xb[:, half * TH : (half + 1) * TH],
                start=(j == 0), stop=(j == NCH - 1), skip_group_check=True),
                waits=wt, auto=False)
            E("tensor", lambda e, j=j, half=half, sqv=sqv: e.matmul(
                ps[2 + half][0:1, 0:TH], ones_c[:],
                sqv[:, half * TH : (half + 1) * TH],
                start=(j == 0), stop=(j == NCH - 1), skip_group_check=True),
                incs=[(st_, 1)], auto=False)
        sd[j] = st_.v

    def ln_finish(dst_bf, spill, eps):
        stats_v = st_.v
        for half in range(2):
            ho = half * TH
            E("vector", lambda e, half=half, ho=ho: e.tensor_scalar_mul(
                mrow[:, ho : ho + TH], ps[half][0:1, 0:TH], 1.0 / D),
              waits=[(st_, stats_v)])
            E("vector", lambda e, half=half, ho=ho, eps=eps: e.tensor_scalar(
                mrow[:, T + ho : T + ho + TH], ps[2 + half][0:1, 0:TH],
                1.0 / D, eps, ALU.mult, ALU.add))
            E("vector", lambda e, ho=ho: e.tensor_tensor(
                aux[0:1, ho : ho + TH], mrow[:, ho : ho + TH], mrow[:, ho : ho + TH], ALU.mult))
            E("vector", lambda e, ho=ho: e.tensor_tensor(
                mrow[:, T + ho : T + ho + TH], mrow[:, T + ho : T + ho + TH],
                aux[0:1, ho : ho + TH], ALU.subtract))
        var_v = sv.v
        E("scalar", lambda e: e.activation(mrow[:, T : 2 * T], mrow[:, T : 2 * T], AF.Ln),
          waits=[(sv, var_v)])
        E("scalar", lambda e: e.activation(mrow[:, T : 2 * T], mrow[:, T : 2 * T], AF.Exp, scale=-0.5))
        rs_v = ss.v
        E("vector", lambda e: e.tensor_tensor(aux[0:1, 0:T], mrow[:, 0:T],
                                              mrow[:, T : 2 * T], ALU.mult),
          waits=[(ss, rs_v)])
        E("vector", lambda e: e.tensor_copy(mbrow[:, 0:T], mrow[:, T : 2 * T]))
        E("vector", lambda e: e.tensor_tensor(mbrow[:, T : 2 * T], mrow[:, T : 2 * T],
                                              mbrow[:, 0:T], ALU.subtract))
        E("vector", lambda e: e.tensor_copy(mbrow[:, 2 * T : 3 * T], aux[0:1, 0:T]))
        rows_v = sv.v
        for half in range(2):
            wt = [(sv, rows_v)] if half == 0 else []
            E("tensor", lambda e, half=half: e.matmul(
                ps[4 + half][:, 0:TH], ones_r[:], mbrow[:, half * TH : half * TH + TH],
                start=True, stop=False, skip_group_check=True), waits=wt, auto=False)
            E("tensor", lambda e, half=half: e.matmul(
                ps[4 + half][:, 0:TH], ones_r[:], mbrow[:, T + half * TH : T + half * TH + TH],
                start=False, stop=True, skip_group_check=True), auto=False)
            E("tensor", lambda e, half=half: e.matmul(
                ps[6 + half][:, 0:TH], ones_r[:], mbrow[:, 2 * T + half * TH : 2 * T + half * TH + TH],
                start=True, stop=True, skip_group_check=True),
                incs=[(st_, 1)] if half == 1 else [], auto=False)
        bc_v = st_.v
        ready = []
        for j in range(NCH):
            E("vector", lambda e, j=j: e.tensor_tensor(
                r2(aux[:]), r2(xres_c(j)), ps2(4), ALU.mult),
                waits=[(st_, bc_v)] + ([(ss, ss.v), (sgd, sgd.v)] if j == 0 else []))
            dst = xnbf_c(j) if dst_bf else uslot(j)
            E("vector", lambda e, dst=dst: e.tensor_tensor(
                r2(dst), r2(aux[:]), ps2(6), ALU.subtract))
            ready.append(sv.v)
            if spill:
                E("gpsimd", lambda e, j=j: e.dma_start(out=xsp_d[j], in_=xres_c(j)),
                  waits=[(sv, sv.v)], incs=[(sgd, 16)], auto=False)
        return sv.v, sgd.v, ready

    def ffn(xn_v, spill_v, start_waits, dbg_tap=False, ln_cb=None, inject=None,
            xn_ready=None):
        ep_done = []
        grp_done = {}
        wl_v = {}

        def emit_gu_load(pair):
            bufp = pair % 2
            wt = [(st_, grp_done[(pair - 2, 1)])] if pair >= 2 else \
                [(scc, W_READY[0])] + list(start_waits)
            if pair in (8, 9, 16, 17):
                wt = wt + [(scc, W_READY[pair // 8])]
            if pair in (24, 25):
                wt = wt + [(scc, W_READY[3])]
            if pair in (40, 41):
                wt = wt + [(scc, W_READY[4])]
            up_eng = "sync" if pair in (24, 25) else "scalar"
            E("sync", lambda e, pair=pair, bufp=bufp: e.dma_start(
                out=wblk[:, bufp * 4096 : bufp * 4096 + 2048], in_=gu_pair_ap(pair, 0)),
                waits=wt, incs=[(sqw[bufp], 16)])
            E(up_eng, lambda e, pair=pair, bufp=bufp: e.dma_start(
                out=wblk[:, bufp * 4096 + 2048 : bufp * 4096 + 4096], in_=gu_pair_ap(pair, 1)),
                waits=wt, incs=[(sqw[bufp], 16)], auto=False)
            wl_v[pair] = sqw[bufp].v

        emit_gu_load(0)
        emit_gu_load(1)
        for pair in range(OP):
            bufp = pair % 2
            if inject:
                for fn in inject.pop(pair, []):
                    fn()
            for half in range(2):
                gb_ = (2 * pair + half) % 2
                ub_ = 2 + (2 * pair + half) % 2
                idx = pair * 2 + half
                wt = [(sqw[pair % 2], wl_v[pair])] if half == 0 else []
                if idx >= 2:
                    wt.append((sv, ep_done[idx - 2]))
                elif xn_ready is None:
                    wt.append((sv, xn_v))
                for which in range(2):
                    bank = gb_ if which == 0 else ub_
                    for k in range(NCH):
                        kw = list(wt) if (which == 0 and k == 0) else []
                        if idx < 2 and xn_ready is not None and which == 0 and half == 0:
                            kw.append((sv, xn_ready[k]))
                        E("tensor", lambda e, bufp=bufp, which=which, k=k, bank=bank, half=half: e.matmul(
                            ps[bank][:, 0:TH],
                            wblk[:, bufp * 4096 + which * 2048 + k * 128 : bufp * 4096 + which * 2048 + (k + 1) * 128],
                            xnbf_c(k, half),
                            start=(k == 0), stop=(k == NCH - 1), skip_group_check=True),
                            waits=kw,
                            incs=[(st_, 1)] if (which == 1 and k == NCH - 1) else [], auto=False)
                grp_done[(pair, half)] = st_.v
                E("scalar", lambda e, gb_=gb_, half=half: e.activation(
                    sb[:, half * TH : (half + 1) * TH], ps[gb_][:, 0:TH], AF.Silu),
                    waits=[(st_, grp_done[(pair, half)])])
                silu_v = ss.v
                hw = [(ss, silu_v)]
                if pair == 22 and half == 0:
                    hw.append((ssp, spill_v["v"]) if isinstance(spill_v, dict)
                              else (sgd, spill_v))
                E("vector", lambda e, pair=pair, half=half, ub_=ub_: e.tensor_tensor(
                    hview(pair, half), sb[:, half * TH : (half + 1) * TH],
                    ps[ub_][:, 0:TH], ALU.mult),
                    waits=hw)
                ep_done.append(sv.v)
            if pair + 2 < OP:
                emit_gu_load(pair + 2)
        GU_DONE = sv.v
        GU_SS = ss.v

        dn_wl = {}
        dn_cons = {}
        ysq_v = {}
        sq_ready = {}

        def emit_dn_load(idx):
            d, kb = idx // 2, idx % 2
            buf = idx % 2
            wt = [(st_, dn_cons[idx - 2])] if idx >= 2 else \
                [(st_, grp_done[(42 + idx, 1)]), (scc, W2_BOX[0])]
            eng = "sync" if idx % 2 == 0 else "scalar"
            E(eng, lambda e, d=d, kb=kb, buf=buf: e.dma_start(
                out=wblk[:, buf * 2816 : (buf + 1) * 2816], in_=dn_blk_ap(d, kb)),
                waits=wt, incs=[(sqw[buf], 16)], auto=False)
            dn_wl[idx] = sqw[buf].v

        emit_dn_load(0)
        emit_dn_load(1)
        for d in range(NCH):
            ybanks = (0, 1) if d % 2 == 0 else (4, 5)
            for kb in range(2):
                buf = (2 * d + kb) % 2
                idx = 2 * d + kb
                for half in range(2):
                    for k in range(22):
                        wt = []
                        if half == 0 and k == 0:
                            wt = [(sqw[buf], dn_wl[idx])]
                            if kb == 0 and d >= 2:
                                wt.append((ss, ysq_v[d - 2]))
                            if kb == 0 and d <= 1:
                                wt.append((sv, GU_DONE))
                                wt.append((ss, GU_SS))
                        E("tensor", lambda e, buf=buf, k=k, kb=kb, half=half, yb=ybanks[half]: e.matmul(
                            ps[yb][:, 0:TH],
                            wblk[:, buf * 2816 + k * 128 : buf * 2816 + (k + 1) * 128],
                            hview(kb * 22 + k, half),
                            start=(kb == 0 and k == 0), stop=(kb == 1 and k == 21),
                            skip_group_check=True),
                            waits=wt,
                            incs=[(st_, 1)] if (half == 1 and k == 21) else [], auto=False)
                dn_cons[idx] = st_.v
                if idx + 2 < 2 * NCH:
                    emit_dn_load(idx + 2)
            mm_v = st_.v
            sqd = mb[:, (d % 2) * T : (d % 2 + 1) * T]
            wy = [(st_, mm_v)]
            if d == 11:
                wy.append((ssp, spill_v["v"]) if isinstance(spill_v, dict)
                          else (sgd, spill_v))
            E("scalar", lambda e, yb0=ybanks[0], sqd=sqd: e.activation(
                r2(sqd[:]), ps2(yb0), AF.Square), waits=wy)
            E("scalar", lambda e, d=d, yb0=ybanks[0]: e.copy(r2(yview(d)), ps2(yb0)))
            sq_ready[d] = ss.v
            ysq_v[d] = ss.v
            for dd in ([d - 1] if d > 0 else []) + ([d] if d == NCH - 1 else []):
                sqp = mb[:, (dd % 2) * T : (dd % 2 + 1) * T]
                for half in range(2):
                    E("tensor", lambda e, dd=dd, half=half, sqp=sqp: e.matmul(
                        ps[2 + half][0:1, 0:TH], ones_c[:], sqp[:, half * TH : (half + 1) * TH],
                        start=(dd == 0), stop=(dd == NCH - 1), skip_group_check=True),
                        waits=[(ss, sq_ready[dd])] if half == 0 else [],
                        incs=[(st_, 1)] if (dd == NCH - 1 and half == 1) else [], auto=False)
        YST_T = st_.v
        Y_EV = ss.v
        if dbg and dbg_tap:
            for d in range(NCH):
                E("gpsimd", lambda e, d=d: e.dma_start(
                    out=dbg_d["h0"][:, d * T : (d + 1) * T], in_=yview(d)),
                    waits=[(ss, Y_EV), (sgd, sgd.v)], incs=[(sgd, 16)], auto=False)
            EW("vector", [(sgd, sgd.v)])
            EW("scalar", [(sgd, sgd.v)])
            EW("gpsimd", [(sgd, sgd.v)])
        for half in range(2):
            E("vector", lambda e, half=half: e.tensor_scalar(
                mrow[:, T + half * TH : T + (half + 1) * TH], ps[2 + half][0:1, 0:TH],
                1.0 / D, RMS_EPS, ALU.mult, ALU.add),
              waits=[(st_, YST_T)])
        E("scalar", lambda e: e.activation(mrow[:, T : 2 * T], mrow[:, T : 2 * T], AF.Ln),
          waits=[(sv, sv.v)])
        E("scalar", lambda e: e.activation(mrow[:, T : 2 * T], mrow[:, T : 2 * T],
                                           AF.Exp, scale=-0.5))
        E("scalar", lambda e: e.mul(mrow[:, T : 2 * T], mrow[:, T : 2 * T], 0.5))
        rs2_v = ss.v
        E("vector", lambda e: e.tensor_copy(mbrow[:, 0:T], mrow[:, T : 2 * T]),
          waits=[(ss, rs2_v)])
        E("vector", lambda e: e.tensor_tensor(mbrow[:, T : 2 * T], mrow[:, T : 2 * T],
                                              mbrow[:, 0:T], ALU.subtract))
        rows_v = sv.v
        bcb = 6 if ln_cb is not None else 2
        for half in range(2):
            wt = [(sv, rows_v)] if half == 0 else []
            E("tensor", lambda e, half=half, bcb=bcb: e.matmul(
                ps[bcb + half][:, 0:TH], ones_r[:], mbrow[:, half * TH : half * TH + TH],
                start=True, stop=False, skip_group_check=True), waits=wt, auto=False)
            E("tensor", lambda e, half=half, bcb=bcb: e.matmul(
                ps[bcb + half][:, 0:TH], ones_r[:], mbrow[:, T + half * TH : T + half * TH + TH],
                start=False, stop=True, skip_group_check=True),
                incs=[(st_, 1)] if half == 1 else [], auto=False)
        bc2_v = st_.v
        xo_v = {}
        for d in range(NCH):
            buf = d % 2
            xob = (U[:, (14 + buf) * T : (15 + buf) * T]) if not small else xoldR[buf]
            wt = [(ss, Y_EV)]
            if d >= 2:
                wt.append((sv, xo_v[d - 2]))
            E("sync", lambda e, d=d, xob=xob: e.dma_start(out=xob[:], in_=xsp_d[d]),
              waits=wt, incs=[(sqo[buf], 16)])
            ld = sqo[buf].v
            E("vector", lambda e, d=d, bcb=bcb: e.tensor_tensor(
                r2(aux[:]), r2(yview(d)), ps2(bcb), ALU.mult),
                waits=[(st_, bc2_v), (ss, Y_EV)])
            E("vector", lambda e, d=d, xob=xob: e.tensor_tensor(
                xres_c(d), aux[:], xob[:], ALU.add),
                waits=[(sqo[buf], ld)])
            xo_v[d] = sv.v
            if ln_cb is not None:
                extra = [(sv, sv.v)]
                if d == 0:
                    extra += [(ss, Y_EV), (st_, bc2_v)]
                ln_cb(d, extra)
        return sv.v, xo_v

    def griffin(xn_v, ln_cb=None, xn_ready=None):
        gwt = wblk[:, 4096:6144] if not small else gatesR[:, 0:2048]
        rwt = wblk[:, 6144:8192] if not small else gatesR[:, 2048:4096]
        def wsrc(ap):
            return ap.rearrange("(h p) c -> h p c", p=128).transpose([1, 0, 2])

        E("gpsimd", lambda e: e.dma_start(
            out=gwt.rearrange("p (h c) -> p h c", h=HEADS), in_=wsrc(gw_in)),
          waits=[(st_, st_.v), (sgd, sgd.v)], incs=[(sgd, 16)], auto=False)
        E("gpsimd", lambda e: e.dma_start(
            out=rwt.rearrange("p (h c) -> p h c", h=HEADS), in_=wsrc(rw_in)),
          incs=[(sgd, 16)], auto=False)
        GW_V = sgd.v
        E("vector", lambda e: e.memset(mf[:, 0:T], 0.0), waits=[])
        Z_V = sv.v

        sig_done = {}
        for h in range(HEADS):
            bq = h % 2
            xnb2 = mb[:, bq * T : (bq + 1) * T]
            wt = [(sv, xn_ready[h] if xn_ready else xn_v)] if h < 2 else \
                [(st_, sig_done[(h - 2, "t")])]
            E("scalar", lambda e, h=h, xnb2=xnb2: e.copy(xnb2, uslot(h)), waits=wt)
            cast_s = ss.v
            bx = 0 if h % 2 == 0 else 4
            for half in range(2):
                for gi in range(2):
                    wtile = gwt if gi == 0 else rwt
                    wt2 = []
                    if half == 0 and gi == 0:
                        wt2 = [(ss, cast_s), (sgd, GW_V)]
                    E("tensor", lambda e, h=h, wtile=wtile, half=half, gi=gi, bx=bx: e.matmul(
                        ps[bx + 2 * gi + half][:, 0:TH],
                        wtile[:, h * 128 : (h + 1) * 128],
                        mb[:, (h % 2) * T + half * TH : (h % 2) * T + (half + 1) * TH],
                        start=True, stop=True, skip_group_check=True),
                        waits=wt2,
                        incs=[(st_, 1)] if (half == 1 and gi == 1) else [], auto=False)
            sig_done[(h, "t")] = st_.v
            E("scalar", lambda e, h=h, bx=bx: e.activation(
                r2(aux[:]), ps2(bx),
                AF.Sigmoid, bias=gvec[:, h : h + 1]),
                waits=[(st_, sig_done[(h, "t")]), (sv, sv.v)])
            E("scalar", lambda e, h=h, bx=bx: e.activation(
                r2(aux2[:]), ps2(bx + 2),
                AF.Sigmoid, bias=gvec[:, HEADS + h : HEADS + h + 1]))
            sig_done[(h, "s")] = ss.v
            E("vector", lambda e, h=h: e.tensor_tensor(
                uslot(b1slot(h)), uslot(h), aux[:], ALU.mult),
                waits=[(ss, sig_done[(h, "s")]), (sv, MSP_V)])
            E("vector", lambda e, h=h: e.tensor_scalar(
                la_f16(h), aux2[:], gvec[:, 3 * HEADS + h : 3 * HEADS + h + 1], None, ALU.mult))
            sig_done[(h, "v")] = sv.v

        bscr = (wblkB[:, 0 : 4 * T].bitcast(F32)) if not small else bscrR[0]
        scanP = {}
        b_done = {}

        a3 = wblkB[:, 4 * T : 8 * T].bitcast(F32)

        def a_slot(h):
            return (aux[:], mf[:, T : 2 * T], a3)[h % 3]

        mb_a2 = mb.bitcast(U8)[:, 0 : 4 * T].bitcast(F32)

        def sweep2_head(h):
            a2 = aux2[:] if h % 2 == 0 else mb_a2[:]
            wt = [(sv, scanP[h - 3])] if h >= 3 else [(sv, sig_done[(HEADS - 1, "v")])]
            E("scalar", lambda e, h=h: e.activation(a_slot(h), la_f16(h), AF.Exp),
              waits=wt)
            E("scalar", lambda e, h=h, a2=a2: e.activation(a2, a_slot(h), AF.Square),
              waits=[(sv, b_done[h - 2])] if h >= 2 else [])
            sq_v = ss.v
            E("vector", lambda e, a2=a2: e.tensor_scalar(a2, a2, -1.0, 1.0, ALU.mult, ALU.add),
              waits=[(ss, sq_v)])
            E("vector", lambda e, a2=a2: e.tensor_scalar_max(a2, a2, 1e-37))
            om_v = sv.v
            E("scalar", lambda e, a2=a2: e.activation(a2, a2, AF.Ln), waits=[(sv, om_v)])
            E("scalar", lambda e, a2=a2: e.activation(a2, a2, AF.Exp, scale=0.5))
            mult_v = ss.v
            E("vector", lambda e, a2=a2: e.scalar_tensor_tensor(
                a2[:, 0:1], a2[:, 0:1], gvec[:, 4 * HEADS + 1 : 4 * HEADS + 2],
                gvec[:, 4 * HEADS : 4 * HEADS + 1], ALU.mult, ALU.add),
                waits=[(ss, mult_v)])
            E("vector", lambda e, h=h, a2=a2: e.tensor_tensor(
                bscr[:, 0:T], uslot(b1slot(h)), a2, ALU.mult))
            b_done[h] = sv.v
            E("vector", lambda e, h=h: e.tensor_tensor_scan(
                uslot(b1slot(h)), a_slot(h), bscr[:, 0:T], 0.0, ALU.mult, ALU.add))
            g = h // 8
            E("vector", lambda e, h=h, g=g: e.tensor_tensor_scan(
                P_bf(h), a_slot(h), mf[:, 0:T], 1.0, ALU.mult, ALU.add),
                waits=[(sv, Z_V)])
            E("vector", lambda e, h=h, g=g: e.tensor_copy(
                cgt[:, g * 16 + (h % 8) : g * 16 + (h % 8) + 1], uslot(b1slot(h))[:, T - 1 : T]))
            E("vector", lambda e, h=h, g=g: e.tensor_copy(
                cgt[:, g * 16 + 8 + (h % 8) : g * 16 + 9 + (h % 8)], P_bf(h)[:, T - 1 : T]))
            scanP[h] = sv.v

        def gather_half(g):
            E("gpsimd", lambda e, g=g: e.dma_start(
                out=cg_in_d[g], in_=cgt[:, g * 16 : (g + 1) * 16]),
              waits=[(sv, scanP[g * 8 + 7]), (sgd, sgd.v)], incs=[(sgd, 16)], auto=False)
            E("gpsimd", lambda e, g=g: e.collective_compute(
                "AllGather", ALU.bypass, ins=[cg_in_d[g]], outs=[cg_out_d[g]],
                replica_groups=[list(range(W))]),
                waits=[(sgd, sgd.v)], incs=[(scc, 1)], auto=False)
            E("gpsimd", lambda e, g=g: e.dma_start(
                out=zg[:, g * W * 16 : (g + 1) * W * 16].rearrange("p (j c) -> p j c", j=W),
                in_=cg_out_d[g].rearrange("(j p) c -> j p c", p=128).transpose([1, 0, 2])),
                waits=[(scc, scc.v)], incs=[(sgd, 16)], auto=False)
            return sgd.v

        def zchain_and_fixups(g, zg_v):
            zo = g * W * 16
            zc = zsc[:, g * 8 : (g + 1) * 8]
            zm = zsc[:, 16 + g * 8 : 16 + (g + 1) * 8]
            E("vector", lambda e, zc=zc: e.memset(zc, 0.0), waits=[(sgd, zg_v)])
            E("vector", lambda e, zm=zm: e.memset(zm, 0.0))
            for j in range(W):
                E("vector", lambda e, j=j, zc=zc, zm=zm: e.scalar_tensor_tensor(
                    zm, zc, zmt[:, j : j + 1], zm, ALU.mult, ALU.add),
                    waits=[(sqb, BIAS_LD)] if j == 0 else [])
                if j < W - 1:
                    E("vector", lambda e, j=j, zo=zo, zc=zc: e.tensor_tensor(
                        zc, zg[:, zo + j * 16 + 8 : zo + j * 16 + 16], zc, ALU.mult))
                    E("vector", lambda e, j=j, zo=zo, zc=zc: e.tensor_tensor(
                        zc, zg[:, zo + j * 16 : zo + j * 16 + 8], zc, ALU.add))
            for hh in range(8):
                h = g * 8 + hh
                E("vector", lambda e, h=h, hh=hh, g=g: e.scalar_tensor_tensor(
                    uslot(b1slot(h)), P_bf(h), zsc[:, 16 + g * 8 + hh : 16 + g * 8 + hh + 1],
                    uslot(b1slot(h)), ALU.mult, ALU.add))
                E("gpsimd", lambda e, h=h: e.tensor_tensor(
                    xres_c(h), xres_c(h), uslot(b1slot(h)), ALU.add),
                    waits=[(sv, sv.v)])
                if ln_cb is not None:
                    extra = [(sgc, sgc.v)]
                    if h == 0:
                        extra += [(ss, ss.v), (st_, st_.v)]
                    ln_cb(h, extra)

        for h in range(8):
            sweep2_head(h)
        zg1 = gather_half(0)
        for h in range(8, 16):
            sweep2_head(h)
        zg2 = gather_half(1)
        zchain_and_fixups(0, zg1)
        zchain_and_fixups(1, zg2)
        return sv.v

    def snap(name, src_fn, wait_v, cast=False):
        if not dbg:
            return
        dst = dbg_d[name]
        for j in range(NCH):
            if cast:
                E("vector", lambda e, j=j: e.tensor_copy(aux[:], src_fn(j)),
                  waits=[(sv, wait_v), (ss, ss.v), (st_, st_.v)])
                E("gpsimd", lambda e, j=j: e.dma_start(
                    out=dst[:, j * T : (j + 1) * T], in_=aux[:]),
                    waits=[(sv, sv.v), (sgd, sgd.v)], incs=[(sgd, 16)], auto=False)
                EW("vector", [(sgd, sgd.v)])
            else:
                E("gpsimd", lambda e, j=j: e.dma_start(
                    out=dst[:, j * T : (j + 1) * T], in_=src_fn(j)),
                    waits=[(sv, wait_v), (ss, ss.v), (st_, st_.v)], incs=[(sgd, 16)], auto=False)
        EW("vector", [(sgd, sgd.v)])
        EW("scalar", [(sgd, sgd.v)])
        EW("gpsimd", [(sgd, sgd.v)])

    snap("xres0", xres_c, XRES_V)
    st1 = ln_begin()
    for j in range(NCH):
        ln_chunk(st1, j, [(sv, XRES_V), (st_, XTM_DONE_T)] if j == 0 else [])
    xn_v, _, xn_rd = ln_finish(True, False, LN_EPS)
    snap("xn1", xnbf_c, xn_v, cast=True)
    st2 = ln_begin()
    x1_v, _ = ffn(xn_v, SPB, [], dbg_tap=True,
                  ln_cb=lambda j, wt: ln_chunk(st2, j, wt),
                  inject=build_inject(), xn_ready=xn_rd)
    if dbg:
        for i, (pr, wh) in enumerate(((0, 0), (1, 0), (24, 0), (0, 1))):
            E("gpsimd", lambda e, i=i, pr=pr, wh=wh: e.dma_start(
                out=dbgw_d[:, i * 2048 : (i + 1) * 2048], in_=gu_pair_ap(pr, wh)),
                waits=[(scc, W_READY[4]), (sgd, sgd.v)], incs=[(sgd, 16)], auto=False)
        for kb in range(2):
            E("gpsimd", lambda e, kb=kb: e.dma_start(
                out=dbgw_d[:, 8192 + kb * 2816 : 8192 + (kb + 1) * 2816], in_=dn_blk_ap(0, kb)),
                waits=[(scc, W2_BOX[0]), (sgd, sgd.v)], incs=[(sgd, 16)], auto=False)
    if dbg:
        for i, ki in enumerate((0, 1, 20, 21)):
            E("vector", lambda e, ki=ki: e.tensor_copy(aux[:], hview(ki)),
              waits=[(sv, x1_v), (ss, ss.v), (st_, st_.v)])
            E("gpsimd", lambda e, i=i: e.dma_start(
                out=dbgh_d[:, i * T : (i + 1) * T], in_=aux[:]),
                waits=[(sv, sv.v), (sgd, sgd.v)], incs=[(sgd, 16)], auto=False)
            EW("vector", [(sgd, sgd.v)])
        EW("scalar", [(sgd, sgd.v)])
        EW("gpsimd", [(sgd, sgd.v)])
    snap("x1", xres_c, x1_v)
    xn2_v, _, xn2_rd = ln_finish(False, False, LN_EPS)
    snap("xn2", uslot, xn2_v)
    st3 = ln_begin()
    x2_v = griffin(xn2_v, ln_cb=lambda j, wt: ln_chunk(st3, j, wt), xn_ready=xn2_rd)
    snap("x2", xres_c, x2_v)
    xn3_v, sp3_v, xn3_rd = ln_finish(True, True, LN_EPS)
    x3_v, x3_rd = ffn(xn3_v, sp3_v, [(sv, x2_v)], xn_ready=xn3_rd)

    if small:
        ytm = ytmR
    else:
        ytm = [U[:, (2 + 2 * i) * T : (2 + 2 * i) * T + 2048] for i in range(2)]
    yev = ss.v
    for r in range(NTT):
        buf = r % 2
        for grp in range(4):
            for j in range(4):
                ch = grp * 4 + j
                wt = []
                if j == 0:
                    wt = [(ss, yev)]
                    if r == 0:
                        wt.append((sv, x3_rd[grp * 4 + 3]))
                    if r >= 2 and grp == 0:
                        wt.append((sout[r % 2], 16 * (r // 2)))
                E("tensor", lambda e, ch=ch, j=j, r=r: e.transpose(
                    ps[7][:, j * 128 : (j + 1) * 128],
                    xres[:, ch * T + r * 128 : ch * T + (r + 1) * 128], identf[:]),
                    waits=wt, incs=[(st_, 1)] if j == 3 else [], auto=False)
            E("scalar", lambda e, buf=buf, grp=grp: e.copy(
                ytm[buf][:, grp * 512 : (grp + 1) * 512], ps[7][:, 0:512]),
                waits=[(st_, st_.v)])
            yev = ss.v
        E("scalar", lambda e, r=r, buf=buf: e.dma_start(
            out=y_out[r * 128 : (r + 1) * 128, :], in_=ytm[buf][:]),
            waits=[(ss, ss.v)], incs=[(sout[buf], 16)], auto=False)
    EW("sync", [(sout[0], 16 * (NTT // 2)), (sout[1], 16 * (NTT // 2))])
    EW("gpsimd", [(sout[0], 16 * (NTT // 2)), (sout[1], 16 * (NTT // 2))])

    with nc.Block() as block:
        @block.sync
        def _(e):
            for f in bb.ops["sync"]:
                f(e)

        @block.vector
        def _(e):
            for f in bb.ops["vector"]:
                f(e)

        @block.scalar
        def _(e):
            for f in bb.ops["scalar"]:
                f(e)

        @block.tensor
        def _(e):
            for f in bb.ops["tensor"]:
                f(e)

        @block.gpsimd
        def _(e):
            for f in bb.ops["gpsimd"]:
                f(e)

    stack.close()
    return nc


def make_in_maps(inputs, T):
    x = np.asarray(inputs["L0_attn_x"])
    rp = np.asarray(inputs["L0_attn_recurrent_param"])
    gw = np.asarray(inputs["L0_attn_input_gate_weight"])
    gb = np.asarray(inputs["L0_attn_input_gate_bias"])
    rw = np.asarray(inputs["L0_attn_recurrent_gate_weight"])
    rb = np.asarray(inputs["L0_attn_recurrent_gate_bias"])
    gup = np.asarray(inputs["L0_ffn_gate_up_packed"])
    gus = np.asarray(inputs["L0_ffn_gate_up_scales"])
    dnp = np.asarray(inputs["L0_ffn_down_packed"])
    dns = np.asarray(inputs["L0_ffn_down_scales"])

    gub = gup.view(np.uint8).reshape(2 * I_FF, GUB)
    dnb = dnp.view(np.uint8).reshape(D, DNB)
    gusT = np.ascontiguousarray(gus.T)
    dnsT = np.ascontiguousarray(dns.T)
    def gu_rows(c):
        idx = []
        pairs = [p for p in range(OP) if p % W == c]
        while len(pairs) < 6:
            pairs.append(0)
        for p in pairs[:6]:
            idx.extend(range(p * 128, (p + 1) * 128))
            idx.extend(range((OP + p) * 128, (OP + p + 1) * 128))
        return np.asarray(idx)
    gwr = np.ascontiguousarray(gw.reshape(D, 128).astype(np.float32))
    rwr = np.ascontiguousarray(rw.reshape(D, 128).astype(np.float32))
    rpr = np.ascontiguousarray(rp.reshape(HEADS, 128).astype(np.float32))
    gbr = np.ascontiguousarray(gb.astype(np.float32))
    rbr = np.ascontiguousarray(rb.astype(np.float32))

    in_maps = []
    for c in range(W):
        fix = np.zeros((128, 2), np.float32)
        fix[:, 0] = 1.0 if c == 0 else 0.0
        fix[:, 1] = 0.0 if c == 0 else 1.0
        zm = np.zeros((128, W), np.float32)
        zm[:, c] = 1.0
        in_maps.append({
            "x_in": np.ascontiguousarray(x[0, c * T : (c + 1) * T, :].astype(np.float32)),
            "gub_in": np.ascontiguousarray(gub[gu_rows(c)]),
            "gus_in": np.ascontiguousarray(gusT[gu_rows(c)]),
            "dnb_in": np.ascontiguousarray(dnb[c * DN_PER_CORE * 128 : (c + 1) * DN_PER_CORE * 128]),
            "dns_in": np.ascontiguousarray(dnsT[c * DN_PER_CORE * 128 : (c + 1) * DN_PER_CORE * 128]),
            "gw_in": gwr, "rw_in": rwr, "gb_in": gbr, "rb_in": rbr, "rp_in": rpr,
            "fix_in": fix, "zm_in": zm,
        })
    return in_maps


def kernel(**inputs):
    T = 1024
    nc = build_program(T)
    in_maps = make_in_maps(inputs, T)
    res = run_bass_kernel_spmd(nc, in_maps, core_ids=list(range(W)))
    out = np.concatenate([r["y_out"] for r in res.results], axis=0)
    return out.reshape(1, W * T, D).astype(np.float32)


# version 23
# speedup vs baseline: 1.0169x; 1.0169x over previous
import numpy as np
from contextlib import ExitStack

import concourse.bass as bass
import concourse.mybir as mybir
from concourse.bass_utils import run_bass_kernel_spmd

F32 = mybir.dt.float32
BF16 = mybir.dt.bfloat16
F16 = mybir.dt.float16
U8 = mybir.dt.uint8
AF = mybir.ActivationFunctionType
ALU = mybir.AluOpType

W = 8
D = 2048
NCH = 16
HEADS = 16
I_FF = 5632
KI = 44
OP = 44
GU_PER_CORE = 12
DN_PER_CORE = 2
GUB = 1024
DNB = 2816
GU_BLK = D * 128
DN_BLK = I_FF * 128
SLICE = GU_PER_CORE * GU_BLK + DN_PER_CORE * DN_BLK
LN_EPS = 1e-5
RMS_EPS = 1e-6
NLN2 = -float(np.log(2.0))


class Sem:
    def __init__(self, h):
        self.h = h
        self.v = 0


class B:

    def __init__(self, nc, ec):
        self.nc = nc
        self.ops = {e: [] for e in ("sync", "vector", "scalar", "tensor", "gpsimd")}
        self.sem = {}
        self.ec = ec

    def S(self, name):
        if name not in self.sem:
            self.sem[name] = Sem(self.ec(self.nc.semaphore(name)))
        return self.sem[name]

    def emit(self, eng, fn, waits=(), incs=(), auto=True):
        own = {"vector": "sv", "scalar": "ss", "gpsimd": "sgc"}.get(eng)
        w2 = [(s, v) for (s, v) in waits if v > 0]
        i2 = []
        if auto and own:
            so = self.S(own)
            w2.append((so, so.v))
            so.v += 1
            i2.append((so, 1))
        for (s, k) in incs:
            s.v += k
            i2.append((s, k))

        def run(e, w2=w2, i2=i2, fn=fn):
            for s, v in w2:
                e.wait_ge(s.h, v)
            ins = fn(e)
            for s, k in i2:
                ins.then_inc(s.h, k)

        self.ops[eng].append(run)

    def wait(self, eng, waits):
        w2 = [(s, v) for (s, v) in waits if v > 0]

        def run(e, w2=w2):
            for s, v in w2:
                e.wait_ge(s.h, v)

        self.ops[eng].append(run)


def build_program(T, dbg=False):
    TH = T // 2
    nc = bass.Bass("TRN2", target_bir_lowering=False)

    x_in = nc.dram_tensor("x_in", [T, D], F32, kind="ExternalInput").ap()
    gub_in = nc.dram_tensor("gub_in", [GU_PER_CORE * 128, GUB], U8, kind="ExternalInput").ap()
    gus_in = nc.dram_tensor("gus_in", [GU_PER_CORE * 128, 16], F16, kind="ExternalInput").ap()
    dnb_in = nc.dram_tensor("dnb_in", [DN_PER_CORE * 128, DNB], U8, kind="ExternalInput").ap()
    dns_in = nc.dram_tensor("dns_in", [DN_PER_CORE * 128, KI], F16, kind="ExternalInput").ap()
    gw_in = nc.dram_tensor("gw_in", [D, 128], F32, kind="ExternalInput").ap()
    rw_in = nc.dram_tensor("rw_in", [D, 128], F32, kind="ExternalInput").ap()
    gb_in = nc.dram_tensor("gb_in", [HEADS, 128], F32, kind="ExternalInput").ap()
    rb_in = nc.dram_tensor("rb_in", [HEADS, 128], F32, kind="ExternalInput").ap()
    rp_in = nc.dram_tensor("rp_in", [HEADS, 128], F32, kind="ExternalInput").ap()
    fix_in = nc.dram_tensor("fix_in", [128, 2], F32, kind="ExternalInput").ap()
    zm_in = nc.dram_tensor("zm_in", [128, W], F32, kind="ExternalInput").ap()
    y_out = nc.dram_tensor("y_out", [T, D], F32, kind="ExternalOutput").ap()

    if dbg:
        dbg_d = {k: nc.dram_tensor(f"dbg_{k}", [128, NCH * T], F32, kind="ExternalOutput").ap()
                 for k in ("xres0", "xn1", "h0", "x1", "xn2", "x2", "xn3")}
        dbgh_d = nc.dram_tensor("dbg_h", [128, 4 * T], F32, kind="ExternalOutput").ap()
        dbgw_d = nc.dram_tensor("dbg_w", [128, 4 * 2048 + 2 * 2816], BF16, kind="ExternalOutput").ap()
    GUPS = [2 * GU_BLK, 2 * GU_BLK, 2 * GU_BLK, 4 * GU_BLK, 2 * GU_BLK]
    DNSL = DN_PER_CORE * DN_BLK
    wslice_gu = [nc.dram_tensor(f"wslice_gu{i}", [GUPS[i]], BF16, kind="Internal").ap()
                 for i in range(5)]
    wslice_dn = nc.dram_tensor("wslice_dn", [DNSL], BF16, kind="Internal").ap()
    wfull_gu = [nc.dram_tensor(f"wfull_gu{i}", [W * GUPS[i]], BF16, kind="Internal").ap()
                for i in range(5)]
    wfull_dn = nc.dram_tensor("wfull_dn", [W * DNSL], BF16, kind="Internal").ap()
    xsp_d = nc.dram_tensor("xsp_d", [NCH, 128, T], F32, kind="Internal").ap()
    cg_in_d = [nc.dram_tensor(f"cg_in{g}", [128, 16], F32, kind="Internal").ap() for g in range(2)]
    cg_out_d = [nc.dram_tensor(f"cg_out{g}", [W * 128, 16], F32, kind="Internal").ap() for g in range(2)]

    def gu_pair_ap(pair, which):
        c, k = pair % W, pair // W
        if k < 3:
            piece, off = k, (c * 2 + which) * GU_BLK
        elif k < 5:
            piece, off = 3, (c * 4 + (k - 3) * 2 + which) * GU_BLK
        else:
            piece, off = 4, (c * 2 + which) * GU_BLK
        return wfull_gu[piece][off : off + GU_BLK].rearrange("(p f) -> p f", p=128)

    def dn_blk_ap(dc, kb):
        off = dc * DN_BLK
        blk = wfull_dn[off : off + DN_BLK].rearrange("(p f) -> p f", p=128)
        return blk[:, kb * (I_FF // 2) : (kb + 1) * (I_FF // 2)]

    stack = ExitStack()
    ec = stack.enter_context
    bb = B(nc, ec)
    E, EW, S = bb.emit, bb.wait, bb.S

    xres = ec(nc.sbuf_tensor("xres", [128, NCH * T], F32))
    U = ec(nc.sbuf_tensor("U", [128, 17 * T], F32))
    xnbf = ec(nc.sbuf_tensor("xnbf", [128, NCH * T], BF16))
    wblk = ec(nc.sbuf_tensor("wblk", [128, 8192], BF16))
    mf = ec(nc.sbuf_tensor("mf", [128, 2 * T], F32))
    mb = ec(nc.sbuf_tensor("mb", [128, 3 * T], BF16))
    sb = ec(nc.sbuf_tensor("sb", [128, T], BF16))
    aux = ec(nc.sbuf_tensor("aux", [128, T], F32))
    aux2 = ec(nc.sbuf_tensor("aux2", [128, T], F32))
    ident = ec(nc.sbuf_tensor("ident", [128, 128], BF16))
    identf = ec(nc.sbuf_tensor("identf", [128, 128], F32))
    ones_c = ec(nc.sbuf_tensor("ones_c", [128, 1], BF16))
    ones_r = ec(nc.sbuf_tensor("ones_r", [1, 128], BF16))
    consts = ec(nc.sbuf_tensor("consts", [128, 4], F32))
    gvec = ec(nc.sbuf_tensor("gvec", [128, 6 * HEADS], F32))
    zg = ec(nc.sbuf_tensor("zg", [128, W * 32], F32))
    zsc = ec(nc.sbuf_tensor("zsc", [128, 48], F32))
    zmt = ec(nc.sbuf_tensor("zmt", [128, W], F32))
    cgt = ec(nc.sbuf_tensor("cgt", [128, 32], F32))
    bias_s = ec(nc.sbuf_tensor("bias_s", [16, 128], F32))
    scl = ec(nc.sbuf_tensor("scl", [128, 280], F16))
    ps_all = ec(nc.psum_tensor("ps_all", [128, 4096], F32))
    ps = [ps_all[:, i * 512 : (i + 1) * 512] for i in range(8)]

    def psw(i):
        return ps_all[:, i * 512 : (i + 2) * 512]

    def ps2(i):
        return ps_all[:, i * 512 : (i + 2) * 512].rearrange(
            "p (b c) -> p b c", b=2)[:, :, 0:TH]

    def r2(ap):
        return ap.rearrange("p (b c) -> p b c", b=2)

    small = T < 1024
    if small:
        decR = ec(nc.sbuf_tensor("decR", [128, 71680], U8))
        xtmR = [ec(nc.sbuf_tensor(f"xtm{i}", [128, 2048], F32)) for i in range(2)]
        gatesR = ec(nc.sbuf_tensor("gatesR", [128, 4096], BF16))
        xoldR = [ec(nc.sbuf_tensor(f"xold{i}", [128, T], F32)) for i in range(2)]
        bscrR = [ec(nc.sbuf_tensor(f"bscr{i}", [128, T], F32)) for i in range(2)]
        ytmR = [ec(nc.sbuf_tensor(f"ytm{i}", [128, 2048], F32)) for i in range(2)]
    Ub = U.bitcast(U8)
    xnbfB = xnbf.bitcast(U8)
    xresB = xres.bitcast(U8)
    wblkB = wblk.bitcast(U8)

    def xres_c(j, half=None):
        if half is None:
            return xres[:, j * T : (j + 1) * T]
        return xres[:, j * T + half * TH : j * T + (half + 1) * TH]

    def uslot(i, half=None):
        if half is None:
            return U[:, i * T : (i + 1) * T]
        return U[:, i * T + half * TH : i * T + (half + 1) * TH]

    def xnbf_c(j, half=None):
        if half is None:
            return xnbf[:, j * T : (j + 1) * T]
        return xnbf[:, j * T + half * TH : j * T + (half + 1) * TH]

    def b1slot(h):
        return (h + 16) % 17

    def hview(ki, half=None):
        base = Ub if ki < 22 else xresB
        k2 = ki if ki < 22 else ki - 22
        v = base[:, k2 * 2 * T : (k2 + 1) * 2 * T].bitcast(BF16)
        if half is None:
            return v
        return v[:, half * TH : (half + 1) * TH]

    def yview(d, half=None):
        if d < 8:
            v = xnbfB[:, d * 4 * T : (d + 1) * 4 * T].bitcast(F32)
        elif d < 11:
            v = uslot(11 + (d - 8))
        else:
            v = xres_c(d)
        if half is None:
            return v
        return v[:, half * TH : (half + 1) * TH]

    def la_f16(h):
        return xnbfB[:, h * 2 * T : (h + 1) * 2 * T].bitcast(F16)

    def P_bf(h):
        return xnbfB[:, h * 2 * T : (h + 1) * 2 * T].bitcast(BF16)

    sv, ss, sgc = S("sv"), S("ss"), S("sgc")
    st_ = S("st_")
    sqx = [S("sqx0"), S("sqx1")]
    sqb = S("sqb")
    sqw = [S("sqw0"), S("sqw1")]
    sqo = [S("sqo0"), S("sqo1")]
    scq = S("scq")
    sqsc = S("sqsc")
    ssp = S("ssp")
    sgd = S("sgd")
    scc = S("scc")
    sout = [S("sout0"), S("sout1")]

    E("gpsimd", lambda e: e.memset(consts[:, 0:1], -2.0))
    E("gpsimd", lambda e: e.memset(consts[:, 1:2], -6.0))
    E("gpsimd", lambda e: e.memset(consts[:, 2:3], 4.0))
    E("gpsimd", lambda e: e.memset(consts[:, 3:4], 64.0))
    E("gpsimd", lambda e: e.memset(ident[:], 0.0))
    E("gpsimd", lambda e: e.affine_select(
        out=ident[:], in_=ident[:], compare_op=ALU.not_equal,
        fill=1.0, base=0, pattern=[[-1, 128]], channel_multiplier=1))
    E("gpsimd", lambda e: e.memset(identf[:], 0.0))
    E("gpsimd", lambda e: e.affine_select(
        out=identf[:], in_=identf[:], compare_op=ALU.not_equal,
        fill=1.0, base=0, pattern=[[-1, 128]], channel_multiplier=1))
    E("vector", lambda e: e.memset(ones_c[:], 1.0))
    E("vector", lambda e: e.memset(ones_r[:], 1.0))
    CONSTS = [(sgc, sgc.v), (sv, sv.v)]

    NTT = T // 128
    if small:
        xtm = xtmR
    else:
        xtm = [U[:, (12 + 2 * i) * T : (12 + 2 * i) * T + 2048] for i in range(2)]

    E("sync", lambda e: e.dma_start(
        out=scl[:, 0:192].rearrange("p (c g) -> p c g", c=12),
        in_=gus_in.rearrange("(c p) g -> p c g", p=128)),
        incs=[(sqsc, 16)], auto=False)
    E("sync", lambda e: e.dma_start(
        out=scl[:, 192:280].rearrange("p (c g) -> p c g", c=2),
        in_=dns_in.rearrange("(c p) g -> p c g", p=128)),
        incs=[(sqsc, 16)], auto=False)
    SCL_W = sqsc.v

    st0 = st_.v
    tile_done_inc = {}
    cnt = 0
    for r in range(max(NTT, 8)):
        if r < NTT:
            cnt += 4
        tile_done_inc[r] = cnt
        if 0 <= r - 2 < 6:
            cnt += 4
    xld_v = {}
    for r in range(NTT):
        buf = r % 2
        w = [(st_, st0 + tile_done_inc[r - 2])] if r >= 2 else []
        E("gpsimd" if r in (1, 3) else "sync",
          lambda e, r=r, buf=buf: e.dma_start(
            out=xtm[buf][:], in_=x_in[r * 128 : (r + 1) * 128, :]),
            waits=w, incs=[(sqx[buf], 16)], auto=False)
        xld_v[r] = sqx[buf].v
    xevac_v = [sv.v]

    def emit_xtile(r):
        buf = r % 2
        ld_v = xld_v[r]
        ev = xevac_v[0]
        for grp in range(4):
            for j in range(4):
                ch = grp * 4 + j
                wt = []
                if j == 0:
                    wt = [(sqx[buf], ld_v), (sv, ev)] + (CONSTS if (r == 0 and grp == 0) else [])
                E("tensor", lambda e, buf=buf, ch=ch, j=j: e.transpose(
                    ps[7][:, j * 128 : (j + 1) * 128],
                    xtm[buf][:, ch * 128 : (ch + 1) * 128], identf[:]),
                    waits=wt, incs=[(st_, 1)] if j == 3 else [], auto=False)
            for j in range(4):
                ch = grp * 4 + j
                E("vector", lambda e, r=r, ch=ch, j=j: e.tensor_copy(
                    xres[:, ch * T + r * 128 : ch * T + (r + 1) * 128],
                    ps[7][:, j * 128 : (j + 1) * 128]),
                    waits=[(st_, st_.v)] if j == 0 else [])
            ev = sv.v
        xevac_v[0] = ev

    dec_base = 0

    def dview(off, nbytes, dt):
        if small:
            v = decR[:, off : off + nbytes]
        else:
            v = Ub[:, off : off + nbytes]
        return v if dt == U8 else v.bitcast(dt)

    def decode_compute(nb, ngrp, bytes_src, scales_src, extra_wait, base, first=False,
                       ext_sct=None, bufv=None, **extra_kw):
        if bufv is not None:
            def dv(off, nbytes, dt):
                v = bufv[:, off : off + nbytes]
                return v if dt == U8 else v.bitcast(dt)
        else:
            dv = dview
        ncdec = 2 * nb
        o = base
        ext_bt = extra_kw.pop("ext_bt", None)
        assert not extra_kw
        bt = ext_bt if ext_bt is not None else dv(o, nb, U8); o += nb
        k_u = dv(o, nb, U8); o += nb
        b3 = dv(o, nb, U8); o += nb
        o = (o + 3) & ~3
        r1 = dv(o, 2 * nb, BF16); o += 2 * nb
        r2 = dv(o, 2 * nb, BF16); o += 2 * nb
        t1 = dv(o, 2 * nb, BF16); o += 2 * nb
        sgn = dv(o, 2 * nb, BF16); o += 2 * nb
        L_ = dv(o, 2 * nb, BF16); o += 2 * nb
        v1 = dv(o, 2 * nb, BF16); o += 2 * nb
        wdec = dv(o, 4 * nb, BF16); o += 4 * nb
        stg = dv(o, 4 * nb, BF16); o += 4 * nb
        sct = dv(o, 2 * ((ngrp + 1) & ~1), F16)

        if ext_bt is None:
            E("scalar", lambda e: e.dma_start(out=bt[:], in_=bytes_src),
              waits=list(extra_wait), incs=[(scq, 16)], auto=False)
        if ext_sct is None:
            E("scalar", lambda e: e.dma_start(out=sct[:, 0:ngrp], in_=scales_src),
              incs=[(scq, 16)], auto=False)
        else:
            sct = ext_sct
        ld_v = scq.v

        for cls in range(2):
            kmask = 0x07 if cls == 0 else 0x70
            smask = 0x08 if cls == 0 else 0x80
            kscale = 0.5 if cls == 0 else (1.0 / 32.0)
            r1s = 0.5 if cls == 0 else (1.0 / 32.0)
            r2s = 1.0 if cls == 0 else (1.0 / 16.0)
            E("vector", lambda e, m=kmask: e.tensor_scalar(k_u[:], bt[:], m, None, ALU.bitwise_and),
              waits=[(scq, ld_v), (ss, ss.v)]
                    + (list(extra_wait) if (ext_bt is not None and cls == 0) else []))
            E("vector", lambda e, m=smask: e.tensor_scalar(b3[:], bt[:], m, None, ALU.bitwise_and))
            gm = sv.v
            E("scalar", lambda e, cls=cls: e.activation(sgn[:], b3[:], AF.Sign, scale=-1.0,
                                                        bias=consts[:, 2 + cls : 3 + cls]),
              waits=[(sv, gm)] + (CONSTS if (first and cls == 0) else []))
            E("scalar", lambda e, s=r1s: e.activation(r1[:], k_u[:], AF.Relu, scale=s, bias=consts[:, 0:1]))
            E("scalar", lambda e, s=r2s: e.activation(r2[:], k_u[:], AF.Relu, scale=s, bias=consts[:, 1:2]))
            E("vector", lambda e: e.tensor_tensor(t1[:], r1[:], r2[:], ALU.add),
              waits=[(ss, ss.v)])
            E("vector", lambda e, s=kscale: e.scalar_tensor_tensor(L_[:], k_u[:], s, t1[:], ALU.mult, ALU.add))
            E("vector", lambda e: e.tensor_tensor(v1[:], L_[:], sgn[:], ALU.mult))
            rep = nb // ngrp
            E("vector", lambda e, cls=cls, rep=rep: e.tensor_tensor(
                wdec[:, cls : ncdec : 2].rearrange("p (g r) -> p g r", g=ngrp),
                v1[:].rearrange("p (g r) -> p g r", g=ngrp),
                sct[:, 0:ngrp].unsqueeze(2).broadcast_to([128, ngrp, rep]),
                ALU.mult),
              waits=[(scq, ld_v)] + ([(sqsc, SCL_W)] if (ext_sct is not None and cls == 0) else []))
        return {"wdec": wdec, "stg": stg, "nt": ncdec // 128, "wdec_v": sv.v}

    def decode_trans(hd, blk_elems, slice_off, store_d, store_ap=None):
        wdec, stg, nt = hd["wdec"], hd["stg"], hd["nt"]
        pbank = [ps[6].bitcast(BF16), ps[4].bitcast(BF16)]
        ev_after = [ss.v, ss.v]
        prev_store = sgd.v
        for grp in range((nt + 3) // 4):
            bk = grp % 2
            tn = min(4, nt - grp * 4)
            for j in range(tn):
                t = grp * 4 + j
                wt = []
                if j == 0:
                    wt = [(sv, hd["wdec_v"]), (ss, ev_after[bk])] + (CONSTS if grp == 0 else [])
                E("tensor", lambda e, t=t, j=j, bk=bk: e.transpose(
                    pbank[bk][:, j * 128 : (j + 1) * 128],
                    wdec[:, t * 128 : (t + 1) * 128], ident[:]),
                    waits=wt, incs=[(st_, 1)] if j == tn - 1 else [], auto=False)
            E("scalar", lambda e, grp=grp, tn=tn, bk=bk: e.copy(
                stg[:, grp * 512 : grp * 512 + tn * 128],
                pbank[bk][:, 0 : tn * 128]),
                waits=[(st_, st_.v)] + ([(sgd, prev_store)] if grp == 0 else []))
            ev_after[bk] = ss.v
        out_ap = store_ap if store_ap is not None else \
            store_d[slice_off : slice_off + blk_elems].rearrange("(p f) -> p f", p=128)
        E("gpsimd", lambda e: e.dma_start(out=out_ap, in_=stg[:, 0 : blk_elems // 128]),
            waits=[(ss, ss.v), (sgd, sgd.v)], incs=[(sgd, 16)], auto=False)

    dec_marks = {0: [], 1: [], 2: []}
    W_READY = {}
    W2_BOX = {}
    dec_buf3 = (decR[:, 47744:71368] if small else xnbfB[:, 0:24064])

    dec_hd = {}

    def emit_dec_compute(lc):
        bs = lc % 3
        dec_hd[lc] = decode_compute(
            GUB, 16, gub_in[lc * 128 : (lc + 1) * 128, :], None,
            dec_marks[bs], base=(0, 24064, 0)[bs], first=(lc == 0),
            ext_sct=scl[:, lc * 16 : (lc + 1) * 16],
            bufv=dec_buf3 if bs == 2 else None)

    def emit_dec_trans(lc):
        decode_trans(dec_hd.pop(lc), GU_BLK, (lc % 2) * GU_BLK, wslice_gu[lc // 2])
        dec_marks[lc % 3] = [(sv, sv.v), (ss, ss.v), (st_, st_.v), (sgd, sgd.v)]
        if lc % 2 == 1:
            rd = lc // 2
            E("gpsimd", lambda e, rd=rd: e.collective_compute(
                "AllGather", ALU.bypass, ins=[wslice_gu[rd]], outs=[wfull_gu[rd]],
                replica_groups=[list(range(W))]),
                waits=[(sgd, sgd.v)], incs=[(scc, 1)], auto=False)
            W_READY[rd] = scc.v

    for r in range(max(NTT, 8)):
        if r < NTT:
            emit_xtile(r)
        if 0 <= r - 2 < 6:
            emit_dec_trans(r - 2)
        if r < 6:
            emit_dec_compute(r)
    XRES_V = xevac_v[0]
    XTM_DONE_T = st_.v

    INJ_BASE = 45056
    INJ_B2 = INJ_BASE + 11840
    inj_hd = [None, None]

    def mk_inj_compute(nb, ngrp, bsrc, sct_ap, base=INJ_BASE, slot=0, ext_bt=None):
        def f():
            ew = [(sv, sv.v), (ss, ss.v), (st_, st_.v), (sgc, sgc.v), (sgd, sgd.v)]
            inj_hd[slot] = decode_compute(nb, ngrp, bsrc, None, ew, base=base,
                                          ext_sct=sct_ap, ext_bt=ext_bt)
        return f

    def mk_inj_trans(blk_elems, off, store_d, store_ap=None, slot=0):
        def f():
            decode_trans(inj_hd[slot], blk_elems, off, store_d, store_ap)
        return f

    def mk_inj_ag(src, dst, box, key):
        def f():
            E("gpsimd", lambda e: e.collective_compute(
                "AllGather", ALU.bypass, ins=[src], outs=[dst],
                replica_groups=[list(range(W))]),
                waits=[(sgd, sgd.v)], incs=[(scc, 1)], auto=False)
            box[key] = scc.v
        return f

    SPB = {"v": 0}

    def build_inject():
        inj = {}

        def add(p, fn):
            inj.setdefault(p, []).append(fn)

        def mk_spill(j):
            def f():
                E("scalar", lambda e, j=j: e.dma_start(out=xsp_d[j], in_=xres_c(j)),
                  waits=[(sv, XRES_V)], incs=[(ssp, 16)], auto=False)
                SPB["v"] = ssp.v
            return f

        for j in range(NCH):
            add(2 + j, mk_spill(j))

        for i2 in range(12):
            lc = 6 + i2 // 2
            hh = i2 % 2
            sl = i2 % 2
            base = INJ_BASE if sl == 0 else INJ_B2
            add(i2, mk_inj_compute(
                GUB // 2, 8,
                gub_in[lc * 128 : (lc + 1) * 128, hh * 512 : (hh + 1) * 512],
                scl[:, lc * 16 + hh * 8 : lc * 16 + hh * 8 + 8],
                base=base, slot=sl))
            pc, po = (3, lc - 6) if lc < 10 else (4, lc - 10)
            full2d = wslice_gu[pc][po * GU_BLK : (po + 1) * GU_BLK].rearrange(
                "(p f) -> p f", p=128)
            add(i2 + 2, mk_inj_trans(GU_BLK // 2, 0, None,
                                     full2d[:, hh * 1024 : (hh + 1) * 1024], slot=sl))
        add(9, mk_inj_ag(wslice_gu[3], wfull_gu[3], W_READY, 3))
        add(13, mk_inj_ag(wslice_gu[4], wfull_gu[4], W_READY, 4))
        mbB_dn = mb.bitcast(U8) if not small else decR

        def dn_preload():
            E("scalar", lambda e: e.dma_start(
                out=mbB_dn[:, 0 : 2 * DNB].rearrange("p (c b) -> p c b", c=2),
                in_=dnb_in.rearrange("(c p) b -> p c b", p=128)),
                waits=[(st_, st_.v), (sv, sv.v), (ss, ss.v)],
                incs=[(scq, 16)], auto=False)

        add(11, dn_preload)
        PCS = [6, 6, 6, 6, 6, 6, 4, 4]
        for j in range(16):
            lc, q = j // 8, j % 8
            cum = sum(PCS[:q])
            g = PCS[q]
            sl = j % 2
            base = INJ_BASE if sl == 0 else INJ_B2
            add(14 + j, mk_inj_compute(
                64 * g, g, None,
                scl[:, 192 + lc * 44 + cum : 192 + lc * 44 + cum + g],
                base=base, slot=sl,
                ext_bt=mbB_dn[:, lc * DNB + cum * 64 : lc * DNB + (cum + g) * 64]))
            chunk2d = wslice_dn[lc * DN_BLK : (lc + 1) * DN_BLK].rearrange(
                "(p f) -> p f", p=128)
            add(16 + j, mk_inj_trans(g * 128 * 128, 0, wslice_dn,
                                     chunk2d[:, cum * 128 : (cum + g) * 128], slot=sl))
        add(31, mk_inj_ag(wslice_dn, wfull_dn, W2_BOX, 0))
        return inj

    E("sync", lambda e: e.dma_start(out=zmt[:], in_=zm_in), incs=[(sqb, 16)])
    E("sync", lambda e: e.dma_start(out=gvec[:, 4 * HEADS : 4 * HEADS + 2], in_=fix_in),
      incs=[(sqb, 16)])
    bsrc = [gb_in, rb_in, rp_in]
    for i in range(3):
        wt = [(st_, st_.v)] if i > 0 else []
        E("sync", lambda e, i=i: e.dma_start(out=bias_s[:], in_=bsrc[i]),
          waits=wt, incs=[(sqb, 16)])
        wt2 = [(sqb, sqb.v)] + (CONSTS + [(sv, sv.v)] if i == 0 else [])
        E("tensor", lambda e, i=i: e.matmul(
            ps[5][:, i * 16 : (i + 1) * 16], bias_s[:],
            identf[0:16, 0:16], start=True, stop=True, skip_group_check=True),
            waits=wt2, incs=[(st_, 1)], auto=False)
    BIAS_LD = sqb.v
    E("vector", lambda e: e.tensor_copy(gvec[:, 0 : 3 * HEADS], ps[5][:, 0 : 3 * HEADS]),
      waits=[(st_, st_.v)])
    GVEC_T = sv.v
    E("scalar", lambda e: e.activation(gvec[:, 3 * HEADS : 4 * HEADS],
                                       gvec[:, 2 * HEADS : 3 * HEADS], AF.Exp),
      waits=[(sv, GVEC_T)])
    E("scalar", lambda e: e.activation(gvec[:, 3 * HEADS : 4 * HEADS],
                                       gvec[:, 3 * HEADS : 4 * HEADS], AF.Ln, bias=1.0))
    E("vector", lambda e: e.tensor_scalar_mul(gvec[:, 3 * HEADS : 4 * HEADS],
                                              gvec[:, 3 * HEADS : 4 * HEADS], -8.0),
      waits=[(ss, ss.v)])
    MSP_V = sv.v

    mrow = mf[0:1, :]
    mbrow = mb[0:1, :]

    def ln_begin():
        return {"stat_done": {}, "last": 0}

    def ln_chunk(st, j, extra_wt):
        sd = st["stat_done"]
        bq = j % 2
        xb = mb[:, bq * T : (bq + 1) * T]
        E("vector", lambda e, j=j, xb=xb: e.tensor_copy(xb, xres_c(j)),
          waits=([(st_, sd[j - 2])] if j >= 2 else []) + list(extra_wt))
        cast_v = sv.v
        sqv = mb[:, 2 * T : 3 * T]
        E("scalar", lambda e, j=j, sqv=sqv: e.activation(sqv, xres_c(j), AF.Square),
          waits=([(st_, sd[j - 1])] if j >= 1 else []) + list(extra_wt))
        sq_v = ss.v
        for half in range(2):
            wt = [(sv, cast_v), (ss, sq_v)] if half == 0 else []
            E("tensor", lambda e, j=j, half=half, xb=xb: e.matmul(
                ps[half][0:1, 0:TH], ones_c[:],
                xb[:, half * TH : (half + 1) * TH],
                start=(j == 0), stop=(j == NCH - 1), skip_group_check=True),
                waits=wt, auto=False)
            E("tensor", lambda e, j=j, half=half, sqv=sqv: e.matmul(
                ps[2 + half][0:1, 0:TH], ones_c[:],
                sqv[:, half * TH : (half + 1) * TH],
                start=(j == 0), stop=(j == NCH - 1), skip_group_check=True),
                incs=[(st_, 1)], auto=False)
        sd[j] = st_.v

    def ln_finish(dst_bf, spill, eps):
        stats_v = st_.v
        for half in range(2):
            ho = half * TH
            E("vector", lambda e, half=half, ho=ho: e.tensor_scalar_mul(
                mrow[:, ho : ho + TH], ps[half][0:1, 0:TH], 1.0 / D),
              waits=[(st_, stats_v)])
            E("vector", lambda e, half=half, ho=ho, eps=eps: e.tensor_scalar(
                mrow[:, T + ho : T + ho + TH], ps[2 + half][0:1, 0:TH],
                1.0 / D, eps, ALU.mult, ALU.add))
            E("vector", lambda e, ho=ho: e.tensor_tensor(
                aux[0:1, ho : ho + TH], mrow[:, ho : ho + TH], mrow[:, ho : ho + TH], ALU.mult))
            E("vector", lambda e, ho=ho: e.tensor_tensor(
                mrow[:, T + ho : T + ho + TH], mrow[:, T + ho : T + ho + TH],
                aux[0:1, ho : ho + TH], ALU.subtract))
        var_v = sv.v
        E("scalar", lambda e: e.activation(mrow[:, T : 2 * T], mrow[:, T : 2 * T], AF.Ln),
          waits=[(sv, var_v)])
        E("scalar", lambda e: e.activation(mrow[:, T : 2 * T], mrow[:, T : 2 * T], AF.Exp, scale=-0.5))
        rs_v = ss.v
        E("vector", lambda e: e.tensor_tensor(aux[0:1, 0:T], mrow[:, 0:T],
                                              mrow[:, T : 2 * T], ALU.mult),
          waits=[(ss, rs_v)])
        E("vector", lambda e: e.tensor_copy(mbrow[:, 0:T], mrow[:, T : 2 * T]))
        E("vector", lambda e: e.tensor_tensor(mbrow[:, T : 2 * T], mrow[:, T : 2 * T],
                                              mbrow[:, 0:T], ALU.subtract))
        E("vector", lambda e: e.tensor_copy(mbrow[:, 2 * T : 3 * T], aux[0:1, 0:T]))
        rows_v = sv.v
        for half in range(2):
            wt = [(sv, rows_v)] if half == 0 else []
            E("tensor", lambda e, half=half: e.matmul(
                ps[4 + half][:, 0:TH], ones_r[:], mbrow[:, half * TH : half * TH + TH],
                start=True, stop=False, skip_group_check=True), waits=wt, auto=False)
            E("tensor", lambda e, half=half: e.matmul(
                ps[4 + half][:, 0:TH], ones_r[:], mbrow[:, T + half * TH : T + half * TH + TH],
                start=False, stop=True, skip_group_check=True), auto=False)
            E("tensor", lambda e, half=half: e.matmul(
                ps[6 + half][:, 0:TH], ones_r[:], mbrow[:, 2 * T + half * TH : 2 * T + half * TH + TH],
                start=True, stop=True, skip_group_check=True),
                incs=[(st_, 1)] if half == 1 else [], auto=False)
        bc_v = st_.v
        ready = []
        for j in range(NCH):
            E("vector", lambda e, j=j: e.tensor_tensor(
                r2(aux[:]), r2(xres_c(j)), ps2(4), ALU.mult),
                waits=[(st_, bc_v)] + ([(ss, ss.v), (sgd, sgd.v)] if j == 0 else []))
            dst = xnbf_c(j) if dst_bf else uslot(j)
            E("vector", lambda e, dst=dst: e.tensor_tensor(
                r2(dst), r2(aux[:]), ps2(6), ALU.subtract))
            ready.append(sv.v)
            if spill:
                E("gpsimd", lambda e, j=j: e.dma_start(out=xsp_d[j], in_=xres_c(j)),
                  waits=[(sv, sv.v)], incs=[(sgd, 16)], auto=False)
        return sv.v, sgd.v, ready

    def ffn(xn_v, spill_v, start_waits, dbg_tap=False, ln_cb=None, inject=None,
            xn_ready=None):
        ep_done = []
        grp_done = {}
        wl_v = {}

        def emit_gu_load(pair):
            bufp = pair % 2
            wt = [(st_, grp_done[(pair - 2, 1)])] if pair >= 2 else \
                [(scc, W_READY[0])] + list(start_waits)
            if pair in (8, 9, 16, 17):
                wt = wt + [(scc, W_READY[pair // 8])]
            if pair in (24, 25):
                wt = wt + [(scc, W_READY[3])]
            if pair in (40, 41):
                wt = wt + [(scc, W_READY[4])]
            up_eng = "sync" if pair in (24, 25) else "scalar"
            E("sync", lambda e, pair=pair, bufp=bufp: e.dma_start(
                out=wblk[:, bufp * 4096 : bufp * 4096 + 2048], in_=gu_pair_ap(pair, 0)),
                waits=wt, incs=[(sqw[bufp], 16)])
            E(up_eng, lambda e, pair=pair, bufp=bufp: e.dma_start(
                out=wblk[:, bufp * 4096 + 2048 : bufp * 4096 + 4096], in_=gu_pair_ap(pair, 1)),
                waits=wt, incs=[(sqw[bufp], 16)], auto=False)
            wl_v[pair] = sqw[bufp].v

        emit_gu_load(0)
        emit_gu_load(1)
        for pair in range(OP):
            bufp = pair % 2
            if inject:
                for fn in inject.pop(pair, []):
                    fn()
            for half in range(2):
                gb_ = (2 * pair + half) % 2
                ub_ = 2 + (2 * pair + half) % 2
                idx = pair * 2 + half
                wt = [(sqw[pair % 2], wl_v[pair])] if half == 0 else []
                if idx >= 2:
                    wt.append((sv, ep_done[idx - 2]))
                elif xn_ready is None:
                    wt.append((sv, xn_v))
                for which in range(2):
                    bank = gb_ if which == 0 else ub_
                    for k in range(NCH):
                        kw = list(wt) if (which == 0 and k == 0) else []
                        if idx < 2 and xn_ready is not None and which == 0 and half == 0:
                            kw.append((sv, xn_ready[k]))
                        E("tensor", lambda e, bufp=bufp, which=which, k=k, bank=bank, half=half: e.matmul(
                            ps[bank][:, 0:TH],
                            wblk[:, bufp * 4096 + which * 2048 + k * 128 : bufp * 4096 + which * 2048 + (k + 1) * 128],
                            xnbf_c(k, half),
                            start=(k == 0), stop=(k == NCH - 1), skip_group_check=True),
                            waits=kw,
                            incs=[(st_, 1)] if (which == 1 and k == NCH - 1) else [], auto=False)
                grp_done[(pair, half)] = st_.v
                E("scalar", lambda e, gb_=gb_, half=half: e.activation(
                    sb[:, half * TH : (half + 1) * TH], ps[gb_][:, 0:TH], AF.Silu),
                    waits=[(st_, grp_done[(pair, half)])])
                silu_v = ss.v
                hw = [(ss, silu_v)]
                if pair == 22 and half == 0:
                    hw.append((ssp, spill_v["v"]) if isinstance(spill_v, dict)
                              else (sgd, spill_v))
                E("vector", lambda e, pair=pair, half=half, ub_=ub_: e.tensor_tensor(
                    hview(pair, half), sb[:, half * TH : (half + 1) * TH],
                    ps[ub_][:, 0:TH], ALU.mult),
                    waits=hw)
                ep_done.append(sv.v)
            if pair + 2 < OP:
                emit_gu_load(pair + 2)
        GU_DONE = sv.v
        GU_SS = ss.v

        dn_wl = {}
        dn_cons = {}
        ysq_v = {}
        sq_ready = {}

        def emit_dn_load(idx):
            d, kb = idx // 2, idx % 2
            buf = idx % 2
            wt = [(st_, dn_cons[idx - 2])] if idx >= 2 else \
                [(st_, grp_done[(42 + idx, 1)]), (scc, W2_BOX[0])]
            eng = "sync" if idx % 2 == 0 else "scalar"
            if idx < 2:
                blk_ap = dn_blk_ap(d, kb)
                E("sync", lambda e, buf=buf, blk_ap=blk_ap: e.dma_start(
                    out=wblk[:, buf * 2816 : buf * 2816 + 1408], in_=blk_ap[:, 0:1408]),
                    waits=wt, incs=[(sqw[buf], 16)], auto=False)
                E("scalar", lambda e, buf=buf, blk_ap=blk_ap: e.dma_start(
                    out=wblk[:, buf * 2816 + 1408 : (buf + 1) * 2816], in_=blk_ap[:, 1408:2816]),
                    waits=wt, incs=[(sqw[buf], 16)], auto=False)
            else:
                E(eng, lambda e, d=d, kb=kb, buf=buf: e.dma_start(
                    out=wblk[:, buf * 2816 : (buf + 1) * 2816], in_=dn_blk_ap(d, kb)),
                    waits=wt, incs=[(sqw[buf], 16)], auto=False)
            dn_wl[idx] = sqw[buf].v

        emit_dn_load(0)
        emit_dn_load(1)
        for d in range(NCH):
            ybanks = (0, 1) if d % 2 == 0 else (4, 5)
            for kb in range(2):
                buf = (2 * d + kb) % 2
                idx = 2 * d + kb
                for half in range(2):
                    for k in range(22):
                        wt = []
                        if half == 0 and k == 0:
                            wt = [(sqw[buf], dn_wl[idx])]
                            if kb == 0 and d >= 2:
                                wt.append((ss, ysq_v[d - 2]))
                            if kb == 0 and d <= 1:
                                wt.append((sv, GU_DONE))
                                wt.append((ss, GU_SS))
                        E("tensor", lambda e, buf=buf, k=k, kb=kb, half=half, yb=ybanks[half]: e.matmul(
                            ps[yb][:, 0:TH],
                            wblk[:, buf * 2816 + k * 128 : buf * 2816 + (k + 1) * 128],
                            hview(kb * 22 + k, half),
                            start=(kb == 0 and k == 0), stop=(kb == 1 and k == 21),
                            skip_group_check=True),
                            waits=wt,
                            incs=[(st_, 1)] if (half == 1 and k == 21) else [], auto=False)
                dn_cons[idx] = st_.v
                if idx + 2 < 2 * NCH:
                    emit_dn_load(idx + 2)
            mm_v = st_.v
            sqd = mb[:, (d % 2) * T : (d % 2 + 1) * T]
            wy = [(st_, mm_v)]
            if d == 11:
                wy.append((ssp, spill_v["v"]) if isinstance(spill_v, dict)
                          else (sgd, spill_v))
            E("scalar", lambda e, yb0=ybanks[0], sqd=sqd: e.activation(
                r2(sqd[:]), ps2(yb0), AF.Square), waits=wy)
            E("scalar", lambda e, d=d, yb0=ybanks[0]: e.copy(r2(yview(d)), ps2(yb0)))
            sq_ready[d] = ss.v
            ysq_v[d] = ss.v
            for dd in ([d - 1] if d > 0 else []) + ([d] if d == NCH - 1 else []):
                sqp = mb[:, (dd % 2) * T : (dd % 2 + 1) * T]
                for half in range(2):
                    E("tensor", lambda e, dd=dd, half=half, sqp=sqp: e.matmul(
                        ps[2 + half][0:1, 0:TH], ones_c[:], sqp[:, half * TH : (half + 1) * TH],
                        start=(dd == 0), stop=(dd == NCH - 1), skip_group_check=True),
                        waits=[(ss, sq_ready[dd])] if half == 0 else [],
                        incs=[(st_, 1)] if (dd == NCH - 1 and half == 1) else [], auto=False)
        YST_T = st_.v
        Y_EV = ss.v
        if dbg and dbg_tap:
            for d in range(NCH):
                E("gpsimd", lambda e, d=d: e.dma_start(
                    out=dbg_d["h0"][:, d * T : (d + 1) * T], in_=yview(d)),
                    waits=[(ss, Y_EV), (sgd, sgd.v)], incs=[(sgd, 16)], auto=False)
            EW("vector", [(sgd, sgd.v)])
            EW("scalar", [(sgd, sgd.v)])
            EW("gpsimd", [(sgd, sgd.v)])
        for half in range(2):
            E("vector", lambda e, half=half: e.tensor_scalar(
                mrow[:, T + half * TH : T + (half + 1) * TH], ps[2 + half][0:1, 0:TH],
                1.0 / D, RMS_EPS, ALU.mult, ALU.add),
              waits=[(st_, YST_T)])
        E("scalar", lambda e: e.activation(mrow[:, T : 2 * T], mrow[:, T : 2 * T], AF.Ln),
          waits=[(sv, sv.v)])
        E("scalar", lambda e: e.activation(mrow[:, T : 2 * T], mrow[:, T : 2 * T],
                                           AF.Exp, scale=-0.5))
        E("scalar", lambda e: e.mul(mrow[:, T : 2 * T], mrow[:, T : 2 * T], 0.5))
        rs2_v = ss.v
        E("vector", lambda e: e.tensor_copy(mbrow[:, 0:T], mrow[:, T : 2 * T]),
          waits=[(ss, rs2_v)])
        E("vector", lambda e: e.tensor_tensor(mbrow[:, T : 2 * T], mrow[:, T : 2 * T],
                                              mbrow[:, 0:T], ALU.subtract))
        rows_v = sv.v
        bcb = 6 if ln_cb is not None else 2
        for half in range(2):
            wt = [(sv, rows_v)] if half == 0 else []
            E("tensor", lambda e, half=half, bcb=bcb: e.matmul(
                ps[bcb + half][:, 0:TH], ones_r[:], mbrow[:, half * TH : half * TH + TH],
                start=True, stop=False, skip_group_check=True), waits=wt, auto=False)
            E("tensor", lambda e, half=half, bcb=bcb: e.matmul(
                ps[bcb + half][:, 0:TH], ones_r[:], mbrow[:, T + half * TH : T + half * TH + TH],
                start=False, stop=True, skip_group_check=True),
                incs=[(st_, 1)] if half == 1 else [], auto=False)
        bc2_v = st_.v
        xo_v = {}
        for d in range(NCH):
            buf = d % 2
            xob = (U[:, (14 + buf) * T : (15 + buf) * T]) if not small else xoldR[buf]
            wt = [(ss, Y_EV)]
            if d >= 2:
                wt.append((sv, xo_v[d - 2]))
            E("sync", lambda e, d=d, xob=xob: e.dma_start(out=xob[:], in_=xsp_d[d]),
              waits=wt, incs=[(sqo[buf], 16)])
            ld = sqo[buf].v
            E("vector", lambda e, d=d, bcb=bcb: e.tensor_tensor(
                r2(aux[:]), r2(yview(d)), ps2(bcb), ALU.mult),
                waits=[(st_, bc2_v), (ss, Y_EV)])
            E("vector", lambda e, d=d, xob=xob: e.tensor_tensor(
                xres_c(d), aux[:], xob[:], ALU.add),
                waits=[(sqo[buf], ld)])
            xo_v[d] = sv.v
            if ln_cb is not None:
                extra = [(sv, sv.v)]
                if d == 0:
                    extra += [(ss, Y_EV), (st_, bc2_v)]
                ln_cb(d, extra)
        return sv.v, xo_v

    def griffin(xn_v, ln_cb=None, xn_ready=None):
        gwt = wblk[:, 4096:6144] if not small else gatesR[:, 0:2048]
        rwt = wblk[:, 6144:8192] if not small else gatesR[:, 2048:4096]
        def wsrc(ap):
            return ap.rearrange("(h p) c -> h p c", p=128).transpose([1, 0, 2])

        E("gpsimd", lambda e: e.dma_start(
            out=gwt.rearrange("p (h c) -> p h c", h=HEADS), in_=wsrc(gw_in)),
          waits=[(st_, st_.v), (sgd, sgd.v)], incs=[(sgd, 16)], auto=False)
        E("gpsimd", lambda e: e.dma_start(
            out=rwt.rearrange("p (h c) -> p h c", h=HEADS), in_=wsrc(rw_in)),
          incs=[(sgd, 16)], auto=False)
        GW_V = sgd.v
        E("vector", lambda e: e.memset(mf[:, 0:T], 0.0), waits=[])
        Z_V = sv.v

        sig_done = {}
        for h in range(HEADS):
            bq = h % 2
            xnb2 = mb[:, bq * T : (bq + 1) * T]
            wt = [(sv, xn_ready[h] if xn_ready else xn_v)] if h < 2 else \
                [(st_, sig_done[(h - 2, "t")])]
            E("scalar", lambda e, h=h, xnb2=xnb2: e.copy(xnb2, uslot(h)), waits=wt)
            cast_s = ss.v
            bx = 0 if h % 2 == 0 else 4
            for half in range(2):
                for gi in range(2):
                    wtile = gwt if gi == 0 else rwt
                    wt2 = []
                    if half == 0 and gi == 0:
                        wt2 = [(ss, cast_s), (sgd, GW_V)]
                    E("tensor", lambda e, h=h, wtile=wtile, half=half, gi=gi, bx=bx: e.matmul(
                        ps[bx + 2 * gi + half][:, 0:TH],
                        wtile[:, h * 128 : (h + 1) * 128],
                        mb[:, (h % 2) * T + half * TH : (h % 2) * T + (half + 1) * TH],
                        start=True, stop=True, skip_group_check=True),
                        waits=wt2,
                        incs=[(st_, 1)] if (half == 1 and gi == 1) else [], auto=False)
            sig_done[(h, "t")] = st_.v
            E("scalar", lambda e, h=h, bx=bx: e.activation(
                r2(aux[:]), ps2(bx),
                AF.Sigmoid, bias=gvec[:, h : h + 1]),
                waits=[(st_, sig_done[(h, "t")]), (sv, sv.v)])
            E("scalar", lambda e, h=h, bx=bx: e.activation(
                r2(aux2[:]), ps2(bx + 2),
                AF.Sigmoid, bias=gvec[:, HEADS + h : HEADS + h + 1]))
            sig_done[(h, "s")] = ss.v
            E("vector", lambda e, h=h: e.tensor_tensor(
                uslot(b1slot(h)), uslot(h), aux[:], ALU.mult),
                waits=[(ss, sig_done[(h, "s")]), (sv, MSP_V)])
            E("vector", lambda e, h=h: e.tensor_scalar(
                la_f16(h), aux2[:], gvec[:, 3 * HEADS + h : 3 * HEADS + h + 1], None, ALU.mult))
            sig_done[(h, "v")] = sv.v

        bscr = (wblkB[:, 0 : 4 * T].bitcast(F32)) if not small else bscrR[0]
        scanP = {}
        b_done = {}

        a3 = wblkB[:, 4 * T : 8 * T].bitcast(F32)

        def a_slot(h):
            return (aux[:], mf[:, T : 2 * T], a3)[h % 3]

        mb_a2 = mb.bitcast(U8)[:, 0 : 4 * T].bitcast(F32)

        def sweep2_head(h):
            a2 = aux2[:] if h % 2 == 0 else mb_a2[:]
            wt = [(sv, scanP[h - 3])] if h >= 3 else [(sv, sig_done[(HEADS - 1, "v")])]
            E("scalar", lambda e, h=h: e.activation(a_slot(h), la_f16(h), AF.Exp),
              waits=wt)
            E("scalar", lambda e, h=h, a2=a2: e.activation(a2, a_slot(h), AF.Square),
              waits=[(sv, b_done[h - 2])] if h >= 2 else [])
            sq_v = ss.v
            E("vector", lambda e, a2=a2: e.tensor_scalar(a2, a2, -1.0, 1.0, ALU.mult, ALU.add),
              waits=[(ss, sq_v)])
            E("vector", lambda e, a2=a2: e.tensor_scalar_max(a2, a2, 1e-37))
            om_v = sv.v
            E("scalar", lambda e, a2=a2: e.activation(a2, a2, AF.Ln), waits=[(sv, om_v)])
            E("scalar", lambda e, a2=a2: e.activation(a2, a2, AF.Exp, scale=0.5))
            mult_v = ss.v
            E("vector", lambda e, a2=a2: e.scalar_tensor_tensor(
                a2[:, 0:1], a2[:, 0:1], gvec[:, 4 * HEADS + 1 : 4 * HEADS + 2],
                gvec[:, 4 * HEADS : 4 * HEADS + 1], ALU.mult, ALU.add),
                waits=[(ss, mult_v)])
            E("vector", lambda e, h=h, a2=a2: e.tensor_tensor(
                bscr[:, 0:T], uslot(b1slot(h)), a2, ALU.mult))
            b_done[h] = sv.v
            E("vector", lambda e, h=h: e.tensor_tensor_scan(
                uslot(b1slot(h)), a_slot(h), bscr[:, 0:T], 0.0, ALU.mult, ALU.add))
            g = h // 8
            E("vector", lambda e, h=h, g=g: e.tensor_tensor_scan(
                P_bf(h), a_slot(h), mf[:, 0:T], 1.0, ALU.mult, ALU.add),
                waits=[(sv, Z_V)])
            E("vector", lambda e, h=h, g=g: e.tensor_copy(
                cgt[:, g * 16 + (h % 8) : g * 16 + (h % 8) + 1], uslot(b1slot(h))[:, T - 1 : T]))
            E("vector", lambda e, h=h, g=g: e.tensor_copy(
                cgt[:, g * 16 + 8 + (h % 8) : g * 16 + 9 + (h % 8)], P_bf(h)[:, T - 1 : T]))
            scanP[h] = sv.v

        def gather_half(g):
            E("gpsimd", lambda e, g=g: e.dma_start(
                out=cg_in_d[g], in_=cgt[:, g * 16 : (g + 1) * 16]),
              waits=[(sv, scanP[g * 8 + 7]), (sgd, sgd.v)], incs=[(sgd, 16)], auto=False)
            E("gpsimd", lambda e, g=g: e.collective_compute(
                "AllGather", ALU.bypass, ins=[cg_in_d[g]], outs=[cg_out_d[g]],
                replica_groups=[list(range(W))]),
                waits=[(sgd, sgd.v)], incs=[(scc, 1)], auto=False)
            E("gpsimd", lambda e, g=g: e.dma_start(
                out=zg[:, g * W * 16 : (g + 1) * W * 16].rearrange("p (j c) -> p j c", j=W),
                in_=cg_out_d[g].rearrange("(j p) c -> j p c", p=128).transpose([1, 0, 2])),
                waits=[(scc, scc.v)], incs=[(sgd, 16)], auto=False)
            return sgd.v

        def zchain_and_fixups(g, zg_v):
            zo = g * W * 16
            zc = zsc[:, g * 8 : (g + 1) * 8]
            zm = zsc[:, 16 + g * 8 : 16 + (g + 1) * 8]
            E("vector", lambda e, zc=zc: e.memset(zc, 0.0), waits=[(sgd, zg_v)])
            E("vector", lambda e, zm=zm: e.memset(zm, 0.0))
            for j in range(W):
                E("vector", lambda e, j=j, zc=zc, zm=zm: e.scalar_tensor_tensor(
                    zm, zc, zmt[:, j : j + 1], zm, ALU.mult, ALU.add),
                    waits=[(sqb, BIAS_LD)] if j == 0 else [])
                if j < W - 1:
                    E("vector", lambda e, j=j, zo=zo, zc=zc: e.tensor_tensor(
                        zc, zg[:, zo + j * 16 + 8 : zo + j * 16 + 16], zc, ALU.mult))
                    E("vector", lambda e, j=j, zo=zo, zc=zc: e.tensor_tensor(
                        zc, zg[:, zo + j * 16 : zo + j * 16 + 8], zc, ALU.add))
            for hh in range(8):
                h = g * 8 + hh
                E("vector", lambda e, h=h, hh=hh, g=g: e.scalar_tensor_tensor(
                    uslot(b1slot(h)), P_bf(h), zsc[:, 16 + g * 8 + hh : 16 + g * 8 + hh + 1],
                    uslot(b1slot(h)), ALU.mult, ALU.add))
                E("gpsimd", lambda e, h=h: e.tensor_tensor(
                    xres_c(h), xres_c(h), uslot(b1slot(h)), ALU.add),
                    waits=[(sv, sv.v)])
                if ln_cb is not None:
                    extra = [(sgc, sgc.v)]
                    if h == 0:
                        extra += [(ss, ss.v), (st_, st_.v)]
                    ln_cb(h, extra)

        for h in range(8):
            sweep2_head(h)
        zg1 = gather_half(0)
        for h in range(8, 16):
            sweep2_head(h)
        zg2 = gather_half(1)
        zchain_and_fixups(0, zg1)
        zchain_and_fixups(1, zg2)
        return sv.v

    def snap(name, src_fn, wait_v, cast=False):
        if not dbg:
            return
        dst = dbg_d[name]
        for j in range(NCH):
            if cast:
                E("vector", lambda e, j=j: e.tensor_copy(aux[:], src_fn(j)),
                  waits=[(sv, wait_v), (ss, ss.v), (st_, st_.v)])
                E("gpsimd", lambda e, j=j: e.dma_start(
                    out=dst[:, j * T : (j + 1) * T], in_=aux[:]),
                    waits=[(sv, sv.v), (sgd, sgd.v)], incs=[(sgd, 16)], auto=False)
                EW("vector", [(sgd, sgd.v)])
            else:
                E("gpsimd", lambda e, j=j: e.dma_start(
                    out=dst[:, j * T : (j + 1) * T], in_=src_fn(j)),
                    waits=[(sv, wait_v), (ss, ss.v), (st_, st_.v)], incs=[(sgd, 16)], auto=False)
        EW("vector", [(sgd, sgd.v)])
        EW("scalar", [(sgd, sgd.v)])
        EW("gpsimd", [(sgd, sgd.v)])

    snap("xres0", xres_c, XRES_V)
    st1 = ln_begin()
    for j in range(NCH):
        ln_chunk(st1, j, [(sv, XRES_V), (st_, XTM_DONE_T)] if j == 0 else [])
    xn_v, _, xn_rd = ln_finish(True, False, LN_EPS)
    snap("xn1", xnbf_c, xn_v, cast=True)
    st2 = ln_begin()
    x1_v, _ = ffn(xn_v, SPB, [], dbg_tap=True,
                  ln_cb=lambda j, wt: ln_chunk(st2, j, wt),
                  inject=build_inject(), xn_ready=xn_rd)
    if dbg:
        for i, (pr, wh) in enumerate(((0, 0), (1, 0), (24, 0), (0, 1))):
            E("gpsimd", lambda e, i=i, pr=pr, wh=wh: e.dma_start(
                out=dbgw_d[:, i * 2048 : (i + 1) * 2048], in_=gu_pair_ap(pr, wh)),
                waits=[(scc, W_READY[4]), (sgd, sgd.v)], incs=[(sgd, 16)], auto=False)
        for kb in range(2):
            E("gpsimd", lambda e, kb=kb: e.dma_start(
                out=dbgw_d[:, 8192 + kb * 2816 : 8192 + (kb + 1) * 2816], in_=dn_blk_ap(0, kb)),
                waits=[(scc, W2_BOX[0]), (sgd, sgd.v)], incs=[(sgd, 16)], auto=False)
    if dbg:
        for i, ki in enumerate((0, 1, 20, 21)):
            E("vector", lambda e, ki=ki: e.tensor_copy(aux[:], hview(ki)),
              waits=[(sv, x1_v), (ss, ss.v), (st_, st_.v)])
            E("gpsimd", lambda e, i=i: e.dma_start(
                out=dbgh_d[:, i * T : (i + 1) * T], in_=aux[:]),
                waits=[(sv, sv.v), (sgd, sgd.v)], incs=[(sgd, 16)], auto=False)
            EW("vector", [(sgd, sgd.v)])
        EW("scalar", [(sgd, sgd.v)])
        EW("gpsimd", [(sgd, sgd.v)])
    snap("x1", xres_c, x1_v)
    xn2_v, _, xn2_rd = ln_finish(False, False, LN_EPS)
    snap("xn2", uslot, xn2_v)
    st3 = ln_begin()
    x2_v = griffin(xn2_v, ln_cb=lambda j, wt: ln_chunk(st3, j, wt), xn_ready=xn2_rd)
    snap("x2", xres_c, x2_v)
    xn3_v, sp3_v, xn3_rd = ln_finish(True, True, LN_EPS)
    x3_v, x3_rd = ffn(xn3_v, sp3_v, [(sv, x2_v)], xn_ready=xn3_rd)

    if small:
        ytm = ytmR
    else:
        ytm = [U[:, (2 + 2 * i) * T : (2 + 2 * i) * T + 2048] for i in range(2)]
    yev = ss.v
    for r in range(NTT):
        buf = r % 2
        for grp in range(4):
            for j in range(4):
                ch = grp * 4 + j
                wt = []
                if j == 0:
                    wt = [(ss, yev)]
                    if r == 0:
                        wt.append((sv, x3_rd[grp * 4 + 3]))
                    if r >= 2 and grp == 0:
                        wt.append((sout[r % 2], 16 * (r // 2)))
                E("tensor", lambda e, ch=ch, j=j, r=r: e.transpose(
                    ps[7][:, j * 128 : (j + 1) * 128],
                    xres[:, ch * T + r * 128 : ch * T + (r + 1) * 128], identf[:]),
                    waits=wt, incs=[(st_, 1)] if j == 3 else [], auto=False)
            E("scalar", lambda e, buf=buf, grp=grp: e.copy(
                ytm[buf][:, grp * 512 : (grp + 1) * 512], ps[7][:, 0:512]),
                waits=[(st_, st_.v)])
            yev = ss.v
        E("scalar", lambda e, r=r, buf=buf: e.dma_start(
            out=y_out[r * 128 : (r + 1) * 128, :], in_=ytm[buf][:]),
            waits=[(ss, ss.v)], incs=[(sout[buf], 16)], auto=False)
    EW("sync", [(sout[0], 16 * (NTT // 2)), (sout[1], 16 * (NTT // 2))])
    EW("gpsimd", [(sout[0], 16 * (NTT // 2)), (sout[1], 16 * (NTT // 2))])

    with nc.Block() as block:
        @block.sync
        def _(e):
            for f in bb.ops["sync"]:
                f(e)

        @block.vector
        def _(e):
            for f in bb.ops["vector"]:
                f(e)

        @block.scalar
        def _(e):
            for f in bb.ops["scalar"]:
                f(e)

        @block.tensor
        def _(e):
            for f in bb.ops["tensor"]:
                f(e)

        @block.gpsimd
        def _(e):
            for f in bb.ops["gpsimd"]:
                f(e)

    stack.close()
    return nc


def make_in_maps(inputs, T):
    x = np.asarray(inputs["L0_attn_x"])
    rp = np.asarray(inputs["L0_attn_recurrent_param"])
    gw = np.asarray(inputs["L0_attn_input_gate_weight"])
    gb = np.asarray(inputs["L0_attn_input_gate_bias"])
    rw = np.asarray(inputs["L0_attn_recurrent_gate_weight"])
    rb = np.asarray(inputs["L0_attn_recurrent_gate_bias"])
    gup = np.asarray(inputs["L0_ffn_gate_up_packed"])
    gus = np.asarray(inputs["L0_ffn_gate_up_scales"])
    dnp = np.asarray(inputs["L0_ffn_down_packed"])
    dns = np.asarray(inputs["L0_ffn_down_scales"])

    gub = gup.view(np.uint8).reshape(2 * I_FF, GUB)
    dnb = dnp.view(np.uint8).reshape(D, DNB)
    gusT = np.ascontiguousarray(gus.T)
    dnsT = np.ascontiguousarray(dns.T)
    def gu_rows(c):
        idx = []
        pairs = [p for p in range(OP) if p % W == c]
        while len(pairs) < 6:
            pairs.append(0)
        for p in pairs[:6]:
            idx.extend(range(p * 128, (p + 1) * 128))
            idx.extend(range((OP + p) * 128, (OP + p + 1) * 128))
        return np.asarray(idx)
    gwr = np.ascontiguousarray(gw.reshape(D, 128).astype(np.float32))
    rwr = np.ascontiguousarray(rw.reshape(D, 128).astype(np.float32))
    rpr = np.ascontiguousarray(rp.reshape(HEADS, 128).astype(np.float32))
    gbr = np.ascontiguousarray(gb.astype(np.float32))
    rbr = np.ascontiguousarray(rb.astype(np.float32))

    in_maps = []
    for c in range(W):
        fix = np.zeros((128, 2), np.float32)
        fix[:, 0] = 1.0 if c == 0 else 0.0
        fix[:, 1] = 0.0 if c == 0 else 1.0
        zm = np.zeros((128, W), np.float32)
        zm[:, c] = 1.0
        in_maps.append({
            "x_in": np.ascontiguousarray(x[0, c * T : (c + 1) * T, :].astype(np.float32)),
            "gub_in": np.ascontiguousarray(gub[gu_rows(c)]),
            "gus_in": np.ascontiguousarray(gusT[gu_rows(c)]),
            "dnb_in": np.ascontiguousarray(dnb[c * DN_PER_CORE * 128 : (c + 1) * DN_PER_CORE * 128]),
            "dns_in": np.ascontiguousarray(dnsT[c * DN_PER_CORE * 128 : (c + 1) * DN_PER_CORE * 128]),
            "gw_in": gwr, "rw_in": rwr, "gb_in": gbr, "rb_in": rbr, "rp_in": rpr,
            "fix_in": fix, "zm_in": zm,
        })
    return in_maps


def kernel(**inputs):
    T = 1024
    nc = build_program(T)
    in_maps = make_in_maps(inputs, T)
    res = run_bass_kernel_spmd(nc, in_maps, core_ids=list(range(W)))
    out = np.concatenate([r["y_out"] for r in res.results], axis=0)
    return out.reshape(1, W * T, D).astype(np.float32)
